# version 01
# baseline (speedup 1.0000x reference)
import math
from contextlib import ExitStack

import numpy as np
import concourse.bass as bass
import concourse.mybir as mybir
from concourse.bass_utils import run_bass_kernel_spmd

F32 = mybir.dt.float32
BF16 = mybir.dt.bfloat16
I32 = mybir.dt.int32
AF = mybir.ActivationFunctionType
OP = mybir.AluOpType
AX = mybir.AxisListType

N, C, H, DH, HID = 2048, 1024, 16, 64, 3072
TT, CCn = 16, 8
SCN = 4
SCH = HID // SCN
HFC = SCH // 128
NK = HFC * 4
NQG, QG = 2, 1024
S127 = 1.0 / 127.0
ALPHA = 0.5 * S127 * S127
EPS = 1e-5
INVSQRT8 = float(8.0 ** -0.5)
INV2PI = float(1.0 / (2.0 * math.pi))
TWOPI = float(2.0 * math.pi)

BASE = 16512
OF_X = BASE
OF_A = BASE + 65536
OF_K = BASE + 98304
OF_F = BASE + 131072
OF_W = BASE + 163840
PK = OF_X + 33280
SB_RESERVE = OF_W + 47360


class Sem:
    def __init__(self, nc, ctx, name):
        self.h = ctx.enter_context(nc.semaphore(name))
        self.n = 0
        self.name = name

    def inc(self, instr, k=1):
        instr.then_inc(self.h, k)
        self.n += k


class Tape:
    def __init__(self):
        self.d = {}
        self.record = True

    def mark(self, label, sem):
        if self.record:
            assert label not in self.d, f"dup label {label}"
            self.d[label] = sem.n
        else:
            assert self.d.get(label) == sem.n, (label, self.d.get(label), sem.n)

    def val(self, label):
        return 0 if self.record else self.d[label]


def build_program(tape, upto="full"):
    nc = bass.Bass("TRN2", target_bir_lowering=False, debug=False)

    x_in = nc.dram_tensor("x", [N, C], F32, kind="ExternalInput").ap()
    qkv_w = nc.dram_tensor("qkv_w", [3 * C, C], F32, kind="ExternalInput").ap()
    proj_w = nc.dram_tensor("proj_w", [C, C], F32, kind="ExternalInput").ap()
    proj_b = nc.dram_tensor("proj_b", [1, C], F32, kind="ExternalInput").ap()
    slopes_b = nc.dram_tensor("slopes_b", [128, H], F32, kind="ExternalInput").ap()
    wg_i = nc.dram_tensor("wg_i", [HID, C], I32, kind="ExternalInput").ap()
    wu_i = nc.dram_tensor("wu_i", [HID, C], I32, kind="ExternalInput").ap()
    wd_i = nc.dram_tensor("wd_i", [C, HID], I32, kind="ExternalInput").ap()
    ones_c = nc.dram_tensor("ones_c", [1, 128], BF16, kind="ExternalInput").ap()
    zeros_c = nc.dram_tensor("zeros_c", [1, 4096], BF16, kind="ExternalInput").ap()
    iota_f = nc.dram_tensor("iota_f", [1, N], F32, kind="ExternalInput").ap()
    iota_b = nc.dram_tensor("iota_b", [1, N], BF16, kind="ExternalInput").ap()
    iota_cl = nc.dram_tensor("iota_cl", [128, 1], F32, kind="ExternalInput").ap()
    mst_c = nc.dram_tensor("mst_c", [128, 2 * N - 1], F32, kind="ExternalInput").ap()
    y_out = nc.dram_tensor("y", [N, C], F32, kind="ExternalOutput").ap()

    wg_bf = nc.dram_tensor("wg_bf", [HID, C], BF16, kind="Internal").ap()
    wu_bf = nc.dram_tensor("wu_bf", [HID, C], BF16, kind="Internal").ap()
    wd_bf = nc.dram_tensor("wd_bf", [C, HID], BF16, kind="Internal").ap()
    qkv_bf = nc.dram_tensor("qkv_bf", [3 * C, C], BF16, kind="Internal").ap()
    proj_bf = nc.dram_tensor("proj_bf", [C, C], BF16, kind="Internal").ap()
    pb_bf = nc.dram_tensor("pb_bf", [1, C], BF16, kind="Internal").ap()
    ln_stage = nc.dram_tensor("ln_stage", [N, C], BF16, kind="Internal").ap()
    qk_stage = nc.dram_tensor("qk_stage", [N, 2 * H * 128], BF16, kind="Internal").ap()
    sc_stage = nc.dram_tensor("sc_stage", [128, 2048], BF16, kind="Internal").ap()
    fr_stage = nc.dram_tensor("fr_stage", [1, 32], F32, kind="Internal").ap()
    x_park = nc.dram_tensor("x_park", [128, TT * C], F32, kind="Internal").ap()
    at_stage = nc.dram_tensor("at_stage", [C, N], BF16, kind="Internal").ap()
    rl_stage = nc.dram_tensor("rl_stage", [H, N], F32, kind="Internal").ap()
    rlr_stage = nc.dram_tensor("rlr_stage", [H, N], F32, kind="Internal").ap()

    at = lambda name, shape, dtype, off: nc.alloc_sbuf_tensor_at(name, shape, dtype, offset=off)

    x_sb = at("x_sb", [128, TT * C], F32, OF_X)
    vP = at("vP", [128, TT * H * 65], BF16, OF_X)
    arena = at("arena", [128, 16384], BF16, OF_A)
    ktok = at("ktok", [128, 16384], BF16, OF_K)
    kall = at("kall", [67, H * N], BF16, OF_A)
    lnfm = at("lnfm", [128, CCn * N], BF16, OF_F)
    qall = at("qall", [67, H * N], BF16, OF_F)
    prin = at("prin", [128, CCn * N], BF16, OF_A)

    slsb = at("slsb", [128, H], F32, PK)
    atmp = at("atmp", [128, 2 * N], BF16, PK + 4416)
    iosb = at("iosb", [1, N], BF16, PK + 12608)
    sqb = at("sqb", [128, 1024], BF16, PK + 16704)
    nq2 = at("nq2", [128, TT * H], F32, PK + 18752)
    nk2 = at("nk2", [128, TT * H], F32, PK + 19776)
    smallf = at("smallf", [128, 192], F32, PK + 20800)
    smallb = at("smallb", [128, 512], BF16, PK + 21568)
    mnegall = at("mnegall", [128, 256], F32, PK + 22592)
    iscr = at("iscr", [128, 512], I32, PK + 23616)
    dtok = at("dtok", [128, 256], F32, PK + 25664)
    sten = at("sten", [128, 2 * QG], F32, PK + 4416)
    wex = at("wex", [128, 2 * QG], BF16, PK + 12608)
    asb = at("asb", [64, QG], BF16, PK + 16704)
    lrow = at("lrow", [65, QG], F32, PK + 18752)

    wgT = at("wgT", [128, CCn * SCH], BF16, OF_W)
    wuT = at("wuT", [128, CCn * SCH], BF16, OF_W + 12288)
    wdT = at("wdT", [128, HFC * C], BF16, OF_W + 24576)
    asil = at("asil", [128, 2 * 512], BF16, OF_W + 36864)
    bn6 = at("bn6", [128, TT * 12], F32, OF_W + 38912)
    mvb = at("mvb", [128, TT * 4], F32, OF_W + 39680)
    wqT = at("wqT", [128, CCn * C], BF16, OF_W)
    t1f = at("t1f", [128, 1024], F32, OF_W + 16384)
    t2f = at("t2f", [128, 1024], F32, OF_W + 20480)
    scc = at("scc", [128, 2048], BF16, OF_W + 24576)
    exs = at("exs", [128, 2 * C], BF16, OF_W + 28672)
    exc = at("exc", [128, 2 * C], BF16, OF_W + 32768)
    msp = at("msp", [128, N], BF16, OF_W + 36864)
    mrows = at("mrows", [128, N], BF16, OF_W + 40960)
    mstw = at("mstw", [128, 3072], F32, OF_W + 32768)
    prjT = at("prjT", [128, CCn * C], BF16, OF_W)
    praw = at("praw", [128, N], BF16, OF_W + 16384)
    rlb = at("rlb", [128, N], F32, OF_W + 20480)
    pbs = at("pbs", [1, C], BF16, OF_W + 28672)
    lsb = at("lsb", [H, N], F32, OF_W + 30720)
    rsb = at("rsb", [H, N], F32, OF_W + 38912)
    onesb = at("onesb", [1, 128], BF16, OF_W + 47104)

    assert OF_W + 47360 <= 229344 - 1024
    nc.sbuf_base = SB_RESERVE
    PS = nc.alloc_psum_tensor("PS", [128, 4096], F32)

    ctx = ExitStack()
    mk = lambda name: Sem(nc, ctx, name)
    d_x = mk("d_x"); d_c = mk("d_c"); d_ln = mk("d_ln"); d_w = mk("d_w")
    d_qk = mk("d_qk"); d_m = mk("d_m"); d_at = mk("d_at"); d_pj = mk("d_pj")
    d_out = mk("d_out"); d_pk = mk("d_pk"); d_fr = mk("d_fr"); d_kx = mk("d_kx")
    g_cast = mk("g_cast"); g_cast2 = mk("g_cast2"); g_cast3 = mk("g_cast3")
    g_exp = mk("g_exp"); g_st = mk("g_st"); g_z = mk("g_z")
    t_mm = mk("t_mm"); t_av = mk("t_av")
    v_ms = mk("v_ms"); v_ev = mk("v_ev"); v_al = mk("v_al")
    s_ms = mk("s_ms"); s_ev = mk("s_ev"); s_w = mk("s_w")

    M = tape.mark
    V = tape.val

    do_attn = upto in ("attn", "attnonly", "full")
    do_ffn2 = upto == "full"
    do_ffn1 = upto in ("ffn1", "attn", "full")

    if upto == "ln1":
        sublayers = [("ln1", None)]
    elif upto == "ffn1":
        sublayers = [("ln1", "ffn1")]
    elif upto == "attn":
        sublayers = [("ln1", "ffn1"), ("ln2", None)]
    elif upto == "attnonly":
        sublayers = [("ln2", None)]
    else:
        sublayers = [("ln1", "ffn1"), ("ln2", None), ("ln3", "ffn2")]

    def is_above(qg, kt):
        return kt * 128 >= (qg + 1) * QG

    def core_prev(qg, h):
        return (qg, h - 1) if h > 0 else (qg - 1, H - 1)

    with nc.Block() as block:

        @block.sync
        def _(sy):
            i = sy.dma_start(out=x_sb[:].rearrange("p (t c) -> p t c", c=C),
                             in_=x_in.rearrange("(t p) c -> p t c", p=128))
            d_x.inc(i, 16)
            i = sy.dma_start(out=onesb[:], in_=ones_c); d_c.inc(i, 16)
            M("consts0", d_c)

            def ln_ffn_sync(lnname, ffnname):
                sy.wait_ge(v_ms.h, V(f"{lnname}_tok_done"))
                i = sy.dma_start(out=ln_stage.rearrange("(t p) c -> p t c", p=128),
                                 in_=arena[:].rearrange("p (t c) -> p t c", c=C))
                d_ln.inc(i, 16)
                sy.wait_ge(d_ln.h, d_ln.n)
                for cc in range(CCn):
                    i = sy.dma_start(out=lnfm[:, cc * N:(cc + 1) * N],
                                     in_=ln_stage[:, cc * 128:(cc + 1) * 128],
                                     transpose=True)
                    d_ln.inc(i, 16)
                M(f"{lnname}_fm_loaded", d_ln)

                if ffnname is not None:
                    sy.wait_ge(g_cast.h, V("ffn_w_cast"))
                    for sc in range(SCN):
                        if sc > 0:
                            sy.wait_ge(t_mm.h, V(f"{ffnname}_gu{sc-1}_done"))
                            sy.wait_ge(t_mm.h, V(f"{ffnname}_dn{sc-1}_done"))
                        base = sc * SCH
                        for cc in range(CCn):
                            i = sy.dma_start(out=wgT[:, cc * SCH:(cc + 1) * SCH],
                                             in_=wg_bf[base:base + SCH, cc * 128:(cc + 1) * 128],
                                             transpose=True)
                            d_w.inc(i, 16)
                        for cc in range(CCn):
                            i = sy.dma_start(out=wuT[:, cc * SCH:(cc + 1) * SCH],
                                             in_=wu_bf[base:base + SCH, cc * 128:(cc + 1) * 128],
                                             transpose=True)
                            d_w.inc(i, 16)
                        for hf in range(HFC):
                            i = sy.dma_start(out=wdT[:, hf * C:(hf + 1) * C],
                                             in_=wd_bf[:, base + hf * 128:base + (hf + 1) * 128],
                                             transpose=True)
                            d_w.inc(i, 16)
                        M(f"{ffnname}_w{sc}", d_w)

            for (lnname, ffnname) in sublayers[:2]:
                ln_ffn_sync(lnname, ffnname)

            if do_attn:
                sy.wait_ge(v_ms.h, V("ln2_tok_done"))
                i = sy.dma_start(out=x_park, in_=x_sb[:]); d_pk.inc(i, 16)
                sy.wait_ge(d_pk.h, d_pk.n)
                i = sy.dma_start(out=slsb[:], in_=slopes_b); d_c.inc(i, 16)
                i = sy.dma_start(out=iosb[:], in_=iota_b); d_c.inc(i, 16)
                i = sy.dma_start(out=smallf[0:1, 0:32], in_=iota_f[0:1, 0:32]); d_c.inc(i, 16)
                i = sy.dma_start(out=smallf[:, 32:33], in_=iota_cl); d_c.inc(i, 16)
                M("prep_consts", d_c)
                sy.wait_ge(s_ms.h, V("fr_done"))
                i = sy.dma_start(out=fr_stage, in_=smallf[0:1, 64:96]); d_fr.inc(i, 16)
                sy.wait_ge(d_fr.h, d_fr.n)
                i = sy.dma_start(out=t2f[:, 0:512],
                                 in_=fr_stage[0:1, :].unsqueeze(1)
                                 .broadcast_to([1, 128 * 16, 32]))
                d_fr.inc(i, 16)
                M("frb_loaded", d_fr)

                sy.wait_ge(g_cast2.h, V("qkv_cast"))
                for gidx, gi in enumerate([2, 1, 0]):
                    if gidx > 0:
                        sy.wait_ge(t_mm.h, V(f"qkv_g{gidx-1}_done"))
                    for cc in range(CCn):
                        i = sy.dma_start(out=wqT[:, cc * C:(cc + 1) * C],
                                         in_=qkv_bf[gi * C:(gi + 1) * C, cc * 128:(cc + 1) * 128],
                                         transpose=True)
                        d_qk.inc(i, 16)
                    M(f"qkv_w{gidx}", d_qk)

                qkv_view = qk_stage.rearrange("(t p) f -> p t f", p=128) \
                    .rearrange("p t (h c) -> p t h c", c=128)
                sy.wait_ge(d_qk.h, d_qk.n)
                sy.wait_ge(g_z.h, 16)
                sy.wait_ge(v_ms.h, V("rope_k_done"))
                for tt in range(TT):
                    i = sy.dma_start(
                        out=qkv_view[:, tt:tt + 1, H:2 * H, 0:64].squeeze(1),
                        in_=ktok[:, tt * C:(tt + 1) * C]
                            .rearrange("p (h d) -> p h d", d=64))
                    d_qk.inc(i, 16)
                sy.wait_ge(v_ms.h, V("rope_q_done"))
                for tt in range(TT):
                    i = sy.dma_start(
                        out=qkv_view[:, tt:tt + 1, 0:H, 0:64].squeeze(1),
                        in_=arena[:, tt * C:(tt + 1) * C]
                            .rearrange("p (h d) -> p h d", d=64))
                    d_qk.inc(i, 16)
                sy.wait_ge(d_qk.h, d_qk.n)
                M("qk_staged", d_qk)

                sy.wait_ge(v_ms.h, V("kmax0"))
                i = sy.dma_start(out=lsb[0:1, 0:2048], in_=smallf[:, 128:144])
                d_kx.inc(i, 16)
                sy.wait_ge(g_z.h, 16)
                for h in range(H):
                    if h >= 2:
                        sy.wait_ge(v_ms.h, V(f"kall_{h-2}_copied"))
                    i = sy.dma_start(out=atmp[:, (h % 2) * N:(h % 2 + 1) * N],
                                     in_=qk_stage[:, (H + h) * 128:(H + h + 1) * 128],
                                     transpose=True)
                    d_qk.inc(i, 16)
                    if h >= 1:
                        M(f"kT_{h-1}", d_qk)
                M(f"kT_{H-1}", d_qk)
                sy.wait_ge(v_ms.h, V(f"kall_{H-1}_copied"))
                for h in range(H):
                    if h >= 2:
                        sy.wait_ge(v_ms.h, V(f"qall_{h-2}_copied"))
                    i = sy.dma_start(out=atmp[:, (h % 2) * N:(h % 2 + 1) * N],
                                     in_=qk_stage[:, h * 128:(h + 1) * 128],
                                     transpose=True)
                    d_qk.inc(i, 16)
                    if h >= 1:
                        M(f"qT_{h-1}", d_qk)
                M(f"qT_{H-1}", d_qk)
                sy.wait_ge(v_ms.h, V(f"qall_{H-1}_copied"))

                sy.wait_ge(v_ms.h, V("msp_done"))
                for tt in range(TT):
                    i = sy.dma_start(out=mrows[:, tt * 128:(tt + 1) * 128],
                                     in_=msp[:, tt * 128:(tt + 1) * 128], transpose=True)
                    d_m.inc(i, 16)
                sy.wait_ge(d_m.h, d_m.n)
                for h in range(H):
                    i = sy.dma_start(out=qall[64:67, h * N:(h + 1) * N],
                                     in_=mrows[8 * h:8 * h + 3, :])
                    d_m.inc(i, 16)
                M("qall_m_done", d_m)

                for qg in range(NQG):
                    if qg > 0:
                        sy.wait_ge(v_al.h, V(f"alibi_{qg-1}_done"))
                    else:
                        sy.wait_ge(d_m.h, d_m.n)
                        sy.wait_ge(d_c.h, d_c.n)
                    i = sy.dma_start(out=mstw[:, 0:3071], in_=mst_c[:, qg * QG:qg * QG + 3071])
                    d_c.inc(i, 16)
                    M(f"mstw_{qg}", d_c)
                    for h in range(H):
                        sy.wait_ge(v_ev.h, V(f"avev_{qg}_{h}"))
                        i = sy.dma_start(out=at_stage[h * 64:(h + 1) * 64, qg * QG:(qg + 1) * QG],
                                         in_=asb[:])
                        d_at.inc(i, 16)
                        sy.wait_ge(s_ev.h, V(f"lev_{qg}_{h}"))
                        i = sy.dma_start(out=rl_stage[h:h + 1, qg * QG:(qg + 1) * QG],
                                         in_=lrow[64:65, :])
                        d_at.inc(i, 16)
                        M(f"atst_{qg}_{h}", d_at)

                sy.wait_ge(d_at.h, d_at.n)
                i = sy.dma_start(out=lsb[:], in_=rl_stage); d_pj.inc(i, 16)
                M("lsb_loaded", d_pj)
                sy.wait_ge(v_ms.h, V("recip_done"))
                i = sy.dma_start(out=rlr_stage, in_=rsb[:]); d_pj.inc(i, 16)
                sy.wait_ge(d_pj.h, d_pj.n)
                sy.wait_ge(t_av.h, V("core_av_done"))
                i = sy.dma_start(out=x_sb[:], in_=x_park); d_pj.inc(i, 16)
                sy.wait_ge(g_cast3.h, V("proj_cast"))
                for cc in range(CCn):
                    i = sy.dma_start(out=prjT[:, cc * C:(cc + 1) * C],
                                     in_=proj_bf[:, cc * 128:(cc + 1) * 128], transpose=True)
                    d_pj.inc(i, 16)
                i = sy.dma_start(out=pbs[:], in_=pb_bf); d_pj.inc(i, 16)
                M("projw_loaded", d_pj)
                sy.wait_ge(d_pj.h, d_pj.n)
                for cc in range(CCn):
                    if cc >= 1:
                        sy.wait_ge(v_ms.h, V(f"prin_{cc-1}_done"))
                    i = sy.dma_start(out=praw[:], in_=at_stage[cc * 128:(cc + 1) * 128, :])
                    d_pj.inc(i, 16)
                    i = sy.dma_start(out=rlb[:],
                                     in_=rlr_stage[2 * cc:2 * cc + 2, :]
                                     .unsqueeze(1).broadcast_to([2, 64, N]))
                    d_pj.inc(i, 16)
                    M(f"prin_{cc}_loaded", d_pj)

            for (lnname, ffnname) in sublayers[2:]:
                ln_ffn_sync(lnname, ffnname)

            endsem, endmark = {
                "ln1": (v_ms, "ln1_out_ready"),
                "ffn1": (v_ev, "ffn1_x_done"),
                "attn": (v_ev, "proj_x_done"),
                "attnonly": (v_ev, "proj_x_done"),
                "full": (v_ev, "ffn2_x_done")}[upto]
            sy.wait_ge(endsem.h, V(endmark))
            i = sy.dma_start(out=y_out.rearrange("(t p) c -> p t c", p=128),
                             in_=x_sb[:].rearrange("p (t c) -> p t c", c=C))
            d_out.inc(i, 16)
            sy.wait_ge(d_out.h, d_out.n)

        @block.gpsimd
        def _(gp):
            if do_ffn1:
                i = gp.dma_start(out=wg_bf, in_=wg_i); g_cast.inc(i, 16)
                i = gp.dma_start(out=wu_bf, in_=wu_i); g_cast.inc(i, 16)
                i = gp.dma_start(out=wd_bf, in_=wd_i); g_cast.inc(i, 16)
            M("ffn_w_cast", g_cast)
            if do_attn:
                i = gp.dma_start(
                    out=qk_stage,
                    in_=zeros_c[0:1, :].broadcast_to([2048, 4096]))
                g_z.inc(i, 16)
                i = gp.dma_start(out=qkv_bf, in_=qkv_w); g_cast2.inc(i, 16)
                M("qkv_cast", g_cast2)
                i = gp.dma_start(out=proj_bf, in_=proj_w); g_cast3.inc(i, 16)
                i = gp.dma_start(out=pb_bf, in_=proj_b); g_cast3.inc(i, 16)
                M("proj_cast", g_cast3)
                gp.wait_ge(v_ms.h, V("scc_done"))
                i = gp.dma_start(out=sc_stage, in_=scc[:]); g_st.inc(i, 16)
                gp.wait_ge(g_st.h, g_st.n)
                for idx in range(2 * TT):
                    which, tt = divmod(idx, TT)
                    par = tt % 2
                    if idx >= 2:
                        pw, pt = divmod(idx - 2, TT)
                        gp.wait_ge(v_ms.h, V(f"rope{pw}_{pt}_used"))
                    i = gp.dma_start(
                        out=exs[:, par * C:(par + 1) * C],
                        in_=sc_stage[0:128, tt * 64:tt * 64 + 64]
                            .unsqueeze(1).broadcast_to([128, H, 64]))
                    g_exp.inc(i, 16)
                    i = gp.dma_start(
                        out=exc[:, par * C:(par + 1) * C],
                        in_=sc_stage[0:128, 1024 + tt * 64:1024 + tt * 64 + 64]
                            .unsqueeze(1).broadcast_to([128, H, 64]))
                    g_exp.inc(i, 16)
                    if idx >= 1:
                        pw, pt = divmod(idx - 1, TT)
                        M(f"exp{pw}_{pt}", g_exp)
                M(f"exp1_{TT-1}", g_exp)

        @block.tensor
        def _(te):
            def ffn_t(ffnname, lnname):
                te.wait_ge(d_ln.h, V(f"{lnname}_fm_loaded"))
                for sc in range(SCN):
                    te.wait_ge(d_w.h, V(f"{ffnname}_w{sc}"))
                    for hf in range(HFC):
                        for tg in range(4):
                            k = hf * 4 + tg
                            par = k % 2
                            gof, uof = par * 512, 1024 + par * 512
                            if k >= 2:
                                te.wait_ge(s_ms.h, V(f"{ffnname}_sil{sc}_{k-2}"))
                                te.wait_ge(v_ev.h, V(f"{ffnname}_h{sc}_{k-2}"))
                            elif sc > 0:
                                te.wait_ge(s_ms.h, V(f"{ffnname}_sil{sc-1}_{k+NK-2}"))
                                te.wait_ge(v_ev.h, V(f"{ffnname}_h{sc-1}_{k+NK-2}"))
                            last = None
                            for cc in range(CCn):
                                last = te.matmul(
                                    out=PS[:, gof:gof + 512],
                                    lhsT=wgT[:, cc * SCH + hf * 128:cc * SCH + (hf + 1) * 128],
                                    rhs=lnfm[:, cc * N + tg * 512:cc * N + (tg + 1) * 512],
                                    start=(cc == 0), stop=(cc == CCn - 1))
                            t_mm.inc(last)
                            M(f"{ffnname}_g{sc}_{k}", t_mm)
                            for cc in range(CCn):
                                last = te.matmul(
                                    out=PS[:, uof:uof + 512],
                                    lhsT=wuT[:, cc * SCH + hf * 128:cc * SCH + (hf + 1) * 128],
                                    rhs=lnfm[:, cc * N + tg * 512:cc * N + (tg + 1) * 512],
                                    start=(cc == 0), stop=(cc == CCn - 1))
                            t_mm.inc(last)
                            M(f"{ffnname}_u{sc}_{k}", t_mm)
                    M(f"{ffnname}_gu{sc}_done", t_mm)
                    for tt in range(TT):
                        par = tt % 2
                        dof = 2048 + par * 1024
                        if tt >= 2:
                            te.wait_ge(v_ev.h, V(f"{ffnname}_dnev{sc}_{tt-2}"))
                        elif sc > 0:
                            te.wait_ge(v_ev.h, V(f"{ffnname}_dnev{sc-1}_{tt+TT-2}"))
                        if tt == 0:
                            te.wait_ge(v_ev.h, V(f"{ffnname}_hsc{sc}_all"))
                        last = None
                        for hf in range(HFC):
                            for half in range(2):
                                last = te.matmul(
                                    out=PS[:, dof + half * 512:dof + (half + 1) * 512],
                                    lhsT=arena[:, hf * N + tt * 128:hf * N + (tt + 1) * 128],
                                    rhs=wdT[:, hf * C + half * 512:hf * C + (half + 1) * 512],
                                    start=(hf == 0), stop=(hf == HFC - 1))
                        t_mm.inc(last)
                        M(f"{ffnname}_dnmm{sc}_{tt}", t_mm)
                    M(f"{ffnname}_dn{sc}_done", t_mm)

            if do_ffn1:
                ffn_t("ffn1", "ln1")

            if do_attn:
                te.wait_ge(d_ln.h, V("ln2_fm_loaded"))
                for gidx in range(3):
                    te.wait_ge(d_qk.h, V(f"qkv_w{gidx}"))
                    for tt in range(TT):
                        par = tt % 2
                        pof = par * 1024
                        if gidx == 0 and tt < 2 and do_ffn1:
                            te.wait_ge(s_ms.h, V(f"ffn1_sil{SCN-1}_{NK-2+tt}"))
                            te.wait_ge(v_ev.h, V(f"ffn1_h{SCN-1}_{NK-2+tt}"))
                        if gidx == 0:
                            pass
                        if tt >= 2 or gidx > 0:
                            pidx = gidx * TT + tt - 2
                            pg, pt = divmod(pidx, TT)
                            te.wait_ge(v_ev.h, V(f"qkv_ev{pg}_{pt}"))
                        last = None
                        for half in range(2):
                            for cc in range(CCn):
                                last = te.matmul(
                                    out=PS[:, pof + half * 512:pof + (half + 1) * 512],
                                    lhsT=lnfm[:, cc * N + tt * 128:cc * N + (tt + 1) * 128],
                                    rhs=wqT[:, cc * C + half * 512:cc * C + (half + 1) * 512],
                                    start=(cc == 0), stop=(cc == CCn - 1))
                        t_mm.inc(last)
                        M(f"qkv_mm{gidx}_{tt}", t_mm)
                    M(f"qkv_g{gidx}_done", t_mm)

                te.wait_ge(v_ms.h, V("kk_ready"))
                te.wait_ge(d_c.h, V("prep_consts"))
                if do_ffn1:
                    te.wait_ge(v_ev.h, V(f"ffn1_dnev{SCN-1}_{TT-1}"))
                last = te.matmul(out=PS[:, 2048:2304], lhsT=onesb[:],
                                 rhs=smallb[0:1, 32:288], start=True, stop=True)
                for tt in range(TT):
                    last = te.matmul(out=PS[:, 2304 + tt * 16:2320 + tt * 16],
                                     lhsT=iosb[0:1, tt * 128:(tt + 1) * 128],
                                     rhs=smallb[0:1, 16:32], start=True, stop=True)
                t_mm.inc(last)
                M("mma_all", t_mm)

                te.wait_ge(d_m.h, V("qall_m_done"))
                te.wait_ge(v_ms.h, V(f"kall_{H-1}_copied"))
                te.wait_ge(v_ms.h, V("kall_ones"))
                te.wait_ge(v_ms.h, V(f"qall_{H-1}_copied"))
                for qg in range(NQG):
                    for h in range(H):
                        for kt in range(TT):
                            par = kt % 2
                            sof = par * 1024
                            if kt >= 2:
                                rq, rh, rkt = qg, h, kt - 2
                            elif not (qg == 0 and h == 0):
                                rq, rh = core_prev(qg, h)
                                rkt = kt + TT - 2
                            else:
                                rq = None
                            if rq is not None:
                                if is_above(rq, rkt):
                                    te.wait_ge(s_w.h, V(f"w_{rq}_{rh}_{rkt}"))
                                else:
                                    te.wait_ge(v_al.h, V(f"al_{rq}_{rh}_{rkt}"))
                            last = None
                            for half in range(2):
                                last = te.matmul(
                                    out=PS[:, sof + half * 512:sof + (half + 1) * 512],
                                    lhsT=kall[:, h * N + kt * 128:h * N + (kt + 1) * 128],
                                    rhs=qall[:, h * N + qg * QG + half * 512:
                                             h * N + qg * QG + (half + 1) * 512],
                                    start=True, stop=True)
                            t_mm.inc(last)
                            M(f"sc_{qg}_{h}_{kt}", t_mm)
                            if kt >= 1:
                                ktp = kt - 1
                                te.wait_ge(s_w.h, V(f"w_{qg}_{h}_{ktp}"))
                                if ktp == 0 and not (qg == 0 and h == 0):
                                    pq, ph = core_prev(qg, h)
                                    te.wait_ge(v_ev.h, V(f"avev_{pq}_{ph}"))
                                    te.wait_ge(s_ev.h, V(f"lev_{pq}_{ph}"))
                                wpar = ktp % 2
                                last = None
                                for half in range(2):
                                    last = te.matmul(
                                        out=PS[0:65, 2048 + half * 512:2048 + (half + 1) * 512],
                                        lhsT=vP[:, ktp * (H * 65) + h * 65:
                                                ktp * (H * 65) + (h + 1) * 65],
                                        rhs=wex[:, wpar * QG + half * 512:
                                                wpar * QG + (half + 1) * 512],
                                        start=(ktp == 0), stop=False)
                                t_av.inc(last)
                                M(f"av_{qg}_{h}_{ktp}", t_av)
                        ktp = TT - 1
                        te.wait_ge(s_w.h, V(f"w_{qg}_{h}_{ktp}"))
                        wpar = ktp % 2
                        last = None
                        for half in range(2):
                            last = te.matmul(
                                out=PS[0:65, 2048 + half * 512:2048 + (half + 1) * 512],
                                lhsT=vP[:, ktp * (H * 65) + h * 65:
                                        ktp * (H * 65) + (h + 1) * 65],
                                rhs=wex[:, wpar * QG + half * 512:
                                        wpar * QG + (half + 1) * 512],
                                start=False, stop=True)
                        t_av.inc(last)
                        M(f"av_{qg}_{h}_done", t_av)
                M("core_av_done", t_av)

                te.wait_ge(d_pj.h, V("projw_loaded"))
                te.wait_ge(v_ms.h, V(f"prin_{CCn-1}_done"))
                for tt in range(TT):
                    par = tt % 2
                    pof = par * 1024
                    if tt >= 2:
                        te.wait_ge(v_ev.h, V(f"projev_{tt-2}"))
                    last = None
                    for half in range(2):
                        for cc in range(CCn):
                            last = te.matmul(
                                out=PS[:, pof + half * 512:pof + (half + 1) * 512],
                                lhsT=prin[:, cc * N + tt * 128:cc * N + (tt + 1) * 128],
                                rhs=prjT[:, cc * C + half * 512:cc * C + (half + 1) * 512],
                                start=(cc == 0), stop=False)
                        last = te.matmul(
                            out=PS[:, pof + half * 512:pof + (half + 1) * 512],
                            lhsT=onesb[:],
                            rhs=pbs[0:1, half * 512:(half + 1) * 512],
                            start=False, stop=True)
                    t_mm.inc(last)
                    M(f"projmm_{tt}", t_mm)

            if do_ffn2:
                ffn_t("ffn2", "ln3")

        @block.vector
        def _(ve):
            def ln_v(lnname, waitsem, waitval):
                ve.wait_ge(waitsem.h, waitval)
                last = None
                for tt in range(TT):
                    for hlf in range(2):
                        last = ve.bn_stats(
                            out=bn6[:, tt * 12 + hlf * 6:tt * 12 + (hlf + 1) * 6],
                            in_=x_sb[:, tt * C + hlf * 512:tt * C + (hlf + 1) * 512])
                v_ms.inc(last); M(f"{lnname}_stats", v_ms)
                ve.wait_ge(v_ms.h, V(f"{lnname}_stats"))
                for tt in range(TT):
                    last = ve.bn_aggr(out=mvb[:, tt * 4:tt * 4 + 2],
                                      in_=bn6[:, tt * 12:(tt + 1) * 12])
                v_ms.inc(last); M(f"{lnname}_aggr", v_ms)
                ve.wait_ge(v_ms.h, V(f"{lnname}_aggr"))
                mv = mvb[:].rearrange("p (t f) -> p t f", f=4)
                last = ve.tensor_scalar(out=mv[:, :, 2:3], in0=mv[:, :, 1:2],
                                        scalar1=EPS, scalar2=None, op0=OP.add)
                v_ms.inc(last); M(f"{lnname}_veps", v_ms)
                ve.wait_ge(s_ms.h, V(f"{lnname}_sqrt"))
                last = ve.reciprocal(out=mv[:, :, 2:3], in_=mv[:, :, 3:4])
                v_ms.inc(last); M(f"{lnname}_rstd", v_ms)
                ve.wait_ge(v_ms.h, V(f"{lnname}_rstd"))
                for tt in range(TT):
                    last = ve.tensor_scalar(out=arena[:, tt * C:(tt + 1) * C],
                                            in0=x_sb[:, tt * C:(tt + 1) * C],
                                            scalar1=mvb[:, tt * 4:tt * 4 + 1],
                                            scalar2=mvb[:, tt * 4 + 2:tt * 4 + 3],
                                            op0=OP.subtract, op1=OP.mult)
                v_ms.inc(last); M(f"{lnname}_tok_done", v_ms)

            def ffn_v(ffnname):
                last = None
                for sc in range(SCN):
                    for hf in range(HFC):
                        for tg in range(4):
                            k = hf * 4 + tg
                            par = k % 2
                            uof = 1024 + par * 512
                            ve.wait_ge(s_ms.h, V(f"{ffnname}_sil{sc}_{k}"))
                            ve.wait_ge(t_mm.h, V(f"{ffnname}_u{sc}_{k}"))
                            if k == 0 and sc > 0:
                                ve.wait_ge(t_mm.h, V(f"{ffnname}_dn{sc-1}_done"))
                            last = ve.tensor_tensor(
                                out=arena[:, hf * N + tg * 512:hf * N + (tg + 1) * 512],
                                in0=asil[:, par * 512:(par + 1) * 512],
                                in1=PS[:, uof:uof + 512], op=OP.mult)
                            v_ev.inc(last)
                            M(f"{ffnname}_h{sc}_{k}", v_ev)
                    M(f"{ffnname}_hsc{sc}_all", v_ev)
                    for tt in range(TT):
                        par = tt % 2
                        dof = 2048 + par * 1024
                        ve.wait_ge(t_mm.h, V(f"{ffnname}_dnmm{sc}_{tt}"))
                        last = ve.scalar_tensor_tensor(
                            out=x_sb[:, tt * C:(tt + 1) * C],
                            in0=PS[:, dof:dof + 1024], scalar=ALPHA,
                            in1=x_sb[:, tt * C:(tt + 1) * C],
                            op0=OP.mult, op1=OP.add)
                        v_ev.inc(last)
                        M(f"{ffnname}_dnev{sc}_{tt}", v_ev)
                M(f"{ffnname}_x_done", v_ev)

            if upto != "attnonly":
                ln_v("ln1", d_x, 16)
            if upto == "ln1":
                ve.wait_ge(v_ms.h, V("ln1_tok_done"))
                last = None
                for tt in range(TT):
                    last = ve.tensor_copy(out=x_sb[:, tt * C:(tt + 1) * C],
                                          in_=arena[:, tt * C:(tt + 1) * C])
                v_ms.inc(last); M("ln1_out_ready", v_ms)
                return

            if do_ffn1:
                ffn_v("ffn1")
                ln_v("ln2", v_ev, V("ffn1_x_done"))
            elif upto == "attnonly":
                ln_v("ln2", d_x, 16)

            if do_attn:
                ve.wait_ge(d_c.h, V("prep_consts"))
                last = ve.tensor_scalar(out=smallf[0:1, 96:128], in0=smallf[0:1, 0:32],
                                        scalar1=float(-math.log(10000.0) / 32.0),
                                        scalar2=None, op0=OP.mult)
                v_ms.inc(last); M("frm_done", v_ms)
                ve.wait_ge(d_fr.h, V("frb_loaded"))
                for tt in range(TT):
                    last = ve.tensor_scalar(out=smallf[:, 33 + tt:34 + tt],
                                            in0=smallf[:, 32:33],
                                            scalar1=float(128 * tt), scalar2=INV2PI,
                                            op0=OP.add, op1=OP.mult)
                v_ms.inc(last); M("tv_done", v_ms)
                ve.wait_ge(v_ms.h, V("tv_done"))
                for tt in range(TT):
                    last = ve.tensor_scalar(out=t1f[:, tt * 32:(tt + 1) * 32],
                                            in0=t2f[:, tt * 32:(tt + 1) * 32],
                                            scalar1=smallf[:, 33 + tt:34 + tt],
                                            scalar2=None, op0=OP.mult)
                v_ms.inc(last); M("ang_done", v_ms)
                ve.wait_ge(v_ms.h, V("ang_done"))
                last = ve.tensor_scalar(out=iscr[:], in0=t1f[:, 0:512],
                                        scalar1=0.5, scalar2=None, op0=OP.add)
                v_ms.inc(last); M("rr1", v_ms)
                ve.wait_ge(v_ms.h, V("rr1"))
                last = ve.tensor_copy(out=t1f[:, 512:1024], in_=iscr[:])
                v_ms.inc(last); M("rr2", v_ms)
                ve.wait_ge(v_ms.h, V("rr2"))
                last = ve.tensor_sub(out=t2f[:, 512:1024], in0=t1f[:, 0:512],
                                     in1=t1f[:, 512:1024])
                v_ms.inc(last); M("rsin_done", v_ms)
                ve.wait_ge(v_ms.h, V("rsin_done"))
                last = ve.tensor_scalar(out=t1f[:, 0:512], in0=t1f[:, 0:512],
                                        scalar1=0.25, scalar2=None, op0=OP.add)
                v_ms.inc(last); M("rc0", v_ms)
                ve.wait_ge(v_ms.h, V("rc0"))
                last = ve.tensor_scalar(out=iscr[:], in0=t1f[:, 0:512],
                                        scalar1=0.5, scalar2=None, op0=OP.add)
                v_ms.inc(last); M("rc1", v_ms)
                ve.wait_ge(v_ms.h, V("rc1"))
                last = ve.tensor_copy(out=t1f[:, 512:1024], in_=iscr[:])
                v_ms.inc(last); M("rc2", v_ms)
                ve.wait_ge(v_ms.h, V("rc2"))
                last = ve.tensor_sub(out=t1f[:, 512:1024], in0=t1f[:, 0:512],
                                     in1=t1f[:, 512:1024])
                v_ms.inc(last); M("rcos_done", v_ms)
                ve.wait_ge(s_ms.h, V("sincos_eval"))
                sv = t1f[:, 0:512].rearrange("p (t j) -> p t j", j=32)
                cv = t2f[:, 0:512].rearrange("p (t j) -> p t j", j=32)
                sccv = scc[:].rearrange("p (k t j) -> p k t j", k=2, j=64)
                ve.tensor_scalar(out=sccv[:, 0:1, :, 0:32].squeeze(1), in0=sv,
                                 scalar1=-INVSQRT8, scalar2=None, op0=OP.mult)
                ve.tensor_scalar(out=sccv[:, 0:1, :, 32:64].squeeze(1), in0=sv,
                                 scalar1=INVSQRT8, scalar2=None, op0=OP.mult)
                ve.tensor_scalar(out=sccv[:, 1:2, :, 0:32].squeeze(1), in0=cv,
                                 scalar1=INVSQRT8, scalar2=None, op0=OP.mult)
                last = ve.tensor_scalar(out=sccv[:, 1:2, :, 32:64].squeeze(1), in0=cv,
                                        scalar1=INVSQRT8, scalar2=None, op0=OP.mult)
                v_ms.inc(last); M("scc_done", v_ms)

                ve.wait_ge(d_pk.h, 16)
                last = None
                for tt in range(TT):
                    par = tt % 2
                    pof = par * 1024
                    ve.wait_ge(t_mm.h, V(f"qkv_mm0_{tt}"))
                    vv = vP[:, tt * H * 65:(tt + 1) * H * 65].rearrange(
                        "p (h d) -> p h d", d=65)
                    ve.memset(vv[:, :, 64:65].squeeze(2), 1.0)
                    pv = PS[:, pof:pof + 1024].rearrange("p (h d) -> p h d", d=64)
                    last = ve.tensor_copy(out=vv[:, :, 0:64], in_=pv)
                    v_ev.inc(last)
                    M(f"qkv_ev0_{tt}", v_ev)

                for which, gsel, dst, n2 in ((0, 1, ktok, nk2), (1, 2, arena, nq2)):
                    if which == 1:
                        ve.wait_ge(d_ln.h, V("ln2_fm_loaded"))
                    for tt in range(TT):
                        par = tt % 2
                        pof = par * 1024
                        ve.wait_ge(t_mm.h, V(f"qkv_mm{gsel}_{tt}"))
                        ve.wait_ge(g_exp.h, V(f"exp{which}_{tt}"))
                        P = PS[:, pof:pof + 1024]
                        cosx = exc[:, par * C:(par + 1) * C]
                        sinx = exs[:, par * C:(par + 1) * C]
                        last = ve.tensor_tensor(out=t1f[:], in0=P, in1=cosx, op=OP.mult)
                        v_ms.inc(last); M(f"ropeA{which}_{tt}", v_ms)
                        Pv = P.rearrange("p (h d) -> p h d", d=64)
                        sxv = sinx.rearrange("p (h d) -> p h d", d=64)
                        t2v = t2f[:].rearrange("p (h d) -> p h d", d=64)
                        ve.tensor_tensor(out=t2v[:, :, 0:32], in0=Pv[:, :, 32:64],
                                         in1=sxv[:, :, 0:32], op=OP.mult)
                        last = ve.tensor_tensor(out=t2v[:, :, 32:64], in0=Pv[:, :, 0:32],
                                                in1=sxv[:, :, 32:64], op=OP.mult)
                        v_ms.inc(last)
                        M(f"rope{which}_{tt}_used", v_ms)
                        ve.wait_ge(v_ms.h, V(f"rope{which}_{tt}_used"))
                        last = ve.tensor_add(out=dst[:, tt * C:(tt + 1) * C],
                                             in0=t1f[:], in1=t2f[:])
                        v_ev.inc(last)
                        M(f"qkv_ev{gsel}_{tt}", v_ev)
                        ve.wait_ge(v_ev.h, V(f"qkv_ev{gsel}_{tt}"))
                        last = ve.tensor_mul(out=sqb[:], in0=dst[:, tt * C:(tt + 1) * C],
                                             in1=dst[:, tt * C:(tt + 1) * C])
                        v_ms.inc(last); M(f"sq{which}_{tt}", v_ms)
                        ve.wait_ge(v_ms.h, V(f"sq{which}_{tt}"))
                        last = ve.tensor_reduce(
                            out=n2[:, tt * H:(tt + 1) * H],
                            in_=sqb[:].rearrange("p (h d) -> p h d", d=64),
                            axis=AX.X, op=OP.add)
                        v_ms.inc(last); M(f"n2{which}_{tt}", v_ms)
                        if which == 1:
                            ve.wait_ge(v_ms.h, V(f"n2{which}_{tt}"))
                            last = ve.tensor_mul(out=sqb[:],
                                                 in0=arena[:, tt * C:(tt + 1) * C],
                                                 in1=ktok[:, tt * C:(tt + 1) * C])
                            v_ms.inc(last); M(f"sqd_{tt}", v_ms)
                            ve.wait_ge(v_ms.h, V(f"sqd_{tt}"))
                            last = ve.tensor_reduce(
                                out=dtok[:, tt * H:(tt + 1) * H],
                                in_=sqb[:].rearrange("p (h d) -> p h d", d=64),
                                axis=AX.X, op=OP.add, negate=True)
                            v_ms.inc(last); M(f"negd_{tt}", v_ms)
                    M("rope_k_done" if which == 0 else "rope_q_done", v_ms)

                last = ve.tensor_reduce(out=smallf[:, 128:144],
                                        in_=nk2[:].rearrange("p (t h) -> p h t", h=H),
                                        axis=AX.X, op=OP.max)
                v_ms.inc(last); M("kmax0", v_ms)
                ve.wait_ge(d_kx.h, 16)
                last = ve.tensor_reduce(
                    out=smallf[0:1, 128:144],
                    in_=lsb[0:1, 0:2048].rearrange("a (p h) -> a h p", h=16),
                    axis=AX.X, op=OP.max)
                v_ms.inc(last); M("kmax1", v_ms)
                ve.wait_ge(v_ms.h, V("kmax1"))
                last = ve.tensor_scalar(out=smallb[0:1, 32:48], in0=smallf[0:1, 128:144],
                                        scalar1=-0.5, scalar2=75.0, op0=OP.mult, op1=OP.add)
                v_ms.inc(last); M("kk1", v_ms)
                last = ve.tensor_scalar(out=smallb[0:1, 16:32], in0=slsb[0:1, :],
                                        scalar1=-8.0, scalar2=0.0, op0=OP.mult, op1=OP.max)
                v_ms.inc(last); M("kk2", v_ms)
                rep = 16
                while rep < 256:
                    ve.wait_ge(v_ms.h, v_ms.n)
                    last = ve.tensor_copy(out=smallb[0:1, 32 + rep:32 + 2 * rep],
                                          in_=smallb[0:1, 32:32 + rep])
                    v_ms.inc(last)
                    M(f"brow_{rep}", v_ms)
                    rep *= 2
                M("kk_ready", v_ms)

                ve.wait_ge(t_mm.h, V("mma_all"))
                last = ve.scalar_tensor_tensor(
                    out=mnegall[:], in0=nq2[:], scalar=-0.5,
                    in1=PS[:, 2048:2304], op0=OP.mult, op1=OP.add)
                v_ms.inc(last); M("mneg1", v_ms)
                ve.wait_ge(v_ms.h, V("mneg1"))
                last = ve.tensor_tensor(out=mnegall[:], in0=mnegall[:], in1=dtok[:],
                                        op=OP.min)
                v_ms.inc(last); M("mneg2", v_ms)
                ve.wait_ge(v_ms.h, V("mneg2"))
                last = ve.tensor_tensor(out=mnegall[:], in0=mnegall[:],
                                        in1=PS[:, 2304:2560], op=OP.subtract)
                v_ms.inc(last); M(f"mneg_{TT-1}", v_ms)
                mn = mnegall[:].rearrange("p (t h) -> p t h", h=H)
                mspv = msp[:].rearrange("p (t h s) -> p t h s", h=H, s=8)
                hi_b = mspv[:, :, :, 0:1].squeeze(3)
                mid_b = mspv[:, :, :, 1:2].squeeze(3)
                lo_b = mspv[:, :, :, 2:3].squeeze(3)
                hi_f = t1f[:, 0:256].rearrange("p (t h) -> p t h", h=H)
                d1 = t1f[:, 256:512].rearrange("p (t h) -> p t h", h=H)
                mid_f = t1f[:, 512:768].rearrange("p (t h) -> p t h", h=H)
                d2 = t1f[:, 768:1024].rearrange("p (t h) -> p t h", h=H)
                ve.wait_ge(v_ms.h, V(f"mneg_{TT-1}"))
                last = ve.tensor_copy(out=hi_b, in_=mn)
                v_ms.inc(last); M("mhi", v_ms)
                ve.wait_ge(v_ms.h, V("mhi"))
                last = ve.tensor_copy(out=hi_f, in_=hi_b)
                v_ms.inc(last); M("mhif", v_ms)
                ve.wait_ge(v_ms.h, V("mhif"))
                last = ve.tensor_sub(out=d1, in0=mn, in1=hi_f)
                v_ms.inc(last); M("md1", v_ms)
                ve.wait_ge(v_ms.h, V("md1"))
                last = ve.tensor_copy(out=mid_b, in_=d1)
                v_ms.inc(last); M("mmid", v_ms)
                ve.wait_ge(v_ms.h, V("mmid"))
                last = ve.tensor_copy(out=mid_f, in_=mid_b)
                v_ms.inc(last); M("mmidf", v_ms)
                ve.wait_ge(v_ms.h, V("mmidf"))
                last = ve.tensor_sub(out=d2, in0=d1, in1=mid_f)
                v_ms.inc(last); M("md2", v_ms)
                ve.wait_ge(v_ms.h, V("md2"))
                ve.tensor_copy(out=lo_b, in_=d2)
                last = ve.memset(mspv[:, :, :, 3:8], 0.0)
                v_ms.inc(last); M("msp_done", v_ms)

                for h in range(H):
                    ve.wait_ge(d_qk.h, V(f"kT_{h}"))
                    last = ve.tensor_copy(out=kall[0:64, h * N:(h + 1) * N],
                                          in_=atmp[0:64, (h % 2) * N:(h % 2 + 1) * N])
                    v_ms.inc(last)
                    M(f"kall_{h}_copied", v_ms)
                last = ve.memset(kall[64:67, :], 1.0)
                v_ms.inc(last); M("kall_ones", v_ms)
                for h in range(H):
                    ve.wait_ge(d_qk.h, V(f"qT_{h}"))
                    last = ve.tensor_copy(out=qall[0:64, h * N:(h + 1) * N],
                                          in_=atmp[0:64, (h % 2) * N:(h % 2 + 1) * N])
                    v_ms.inc(last)
                    M(f"qall_{h}_copied", v_ms)

                for qg in range(NQG):
                    ve.wait_ge(d_c.h, V(f"mstw_{qg}"))
                    for h in range(H):
                        for kt in range(TT):
                            if is_above(qg, kt):
                                continue
                            par = kt % 2
                            sof = par * 1024
                            ve.wait_ge(t_mm.h, V(f"sc_{qg}_{h}_{kt}"))
                            if kt >= 2:
                                ve.wait_ge(s_w.h, V(f"w_{qg}_{h}_{kt-2}"))
                            elif not (qg == 0 and h == 0):
                                pq, ph = core_prev(qg, h)
                                ve.wait_ge(s_w.h, V(f"w_{pq}_{ph}_{kt+TT-2}"))
                            cst = 2047 - kt * 128
                            last = ve.scalar_tensor_tensor(
                                out=sten[:, par * QG:(par + 1) * QG],
                                in0=mstw[:, cst:cst + QG],
                                scalar=slsb[:, h:h + 1],
                                in1=PS[:, sof:sof + QG],
                                op0=OP.mult, op1=OP.add)
                            v_al.inc(last)
                            M(f"al_{qg}_{h}_{kt}", v_al)
                        ve.wait_ge(t_av.h, V(f"av_{qg}_{h}_done"))
                        if not (qg == 0 and h == 0):
                            pq, ph = core_prev(qg, h)
                            ve.wait_ge(d_at.h, V(f"atst_{pq}_{ph}"))
                        last = ve.tensor_copy(out=asb[:], in_=PS[0:64, 2048:3072])
                        v_ev.inc(last)
                        M(f"avev_{qg}_{h}", v_ev)
                    M(f"alibi_{qg}_done", v_al)

                ve.wait_ge(d_pj.h, V("lsb_loaded"))
                last = ve.reciprocal(out=rsb[:], in_=lsb[:])
                v_ms.inc(last); M("recip_done", v_ms)

                for cc in range(CCn):
                    ve.wait_ge(d_pj.h, V(f"prin_{cc}_loaded"))
                    last = ve.tensor_tensor(out=prin[:, cc * N:(cc + 1) * N],
                                            in0=praw[:], in1=rlb[:], op=OP.mult)
                    v_ms.inc(last)
                    M(f"prin_{cc}_done", v_ms)
                for tt in range(TT):
                    par = tt % 2
                    pof = par * 1024
                    ve.wait_ge(t_mm.h, V(f"projmm_{tt}"))
                    last = ve.tensor_tensor(out=x_sb[:, tt * C:(tt + 1) * C],
                                            in0=x_sb[:, tt * C:(tt + 1) * C],
                                            in1=PS[:, pof:pof + 1024], op=OP.add)
                    v_ev.inc(last)
                    M(f"projev_{tt}", v_ev)
                M("proj_x_done", v_ev)

            if do_ffn2:
                ln_v("ln3", v_ev, V("proj_x_done"))
                ffn_v("ffn2")

        @block.scalar
        def _(sc):
            def ln_s(lnname):
                sc.wait_ge(v_ms.h, V(f"{lnname}_veps"))
                mv = mvb[:].rearrange("p (t f) -> p t f", f=4)
                last = sc.activation(out=mv[:, :, 3:4], in_=mv[:, :, 2:3], func=AF.Sqrt)
                s_ms.inc(last)
                M(f"{lnname}_sqrt", s_ms)

            def ffn_s(ffnname):
                for scn in range(SCN):
                    for hf in range(HFC):
                        for tg in range(4):
                            k = hf * 4 + tg
                            par = k % 2
                            gof = par * 512
                            sc.wait_ge(t_mm.h, V(f"{ffnname}_g{scn}_{k}"))
                            if k >= 2:
                                sc.wait_ge(v_ev.h, V(f"{ffnname}_h{scn}_{k-2}"))
                            elif scn > 0:
                                sc.wait_ge(v_ev.h, V(f"{ffnname}_h{scn-1}_{k+NK-2}"))
                            last = sc.activation(out=asil[:, par * 512:(par + 1) * 512],
                                                 in_=PS[:, gof:gof + 512],
                                                 func=AF.Silu, scale=S127)
                            s_ms.inc(last)
                            M(f"{ffnname}_sil{scn}_{k}", s_ms)

            if upto != "attnonly":
                ln_s("ln1")
            if upto == "ln1":
                return
            if do_ffn1:
                ffn_s("ffn1")
                ln_s("ln2")
            elif upto == "attnonly":
                ln_s("ln2")
            if do_attn:
                sc.wait_ge(v_ms.h, V("frm_done"))
                last = sc.activation(out=smallf[0:1, 64:96], in_=smallf[0:1, 96:128],
                                     func=AF.Exp)
                s_ms.inc(last); M("fr_done", s_ms)
                sc.wait_ge(v_ms.h, V("rcos_done"))
                last = sc.activation(out=t1f[:, 0:512], in_=t2f[:, 512:1024],
                                     func=AF.Sin, scale=TWOPI)
                s_ms.inc(last); M("sin_eval", s_ms)
                last = sc.activation(out=t2f[:, 0:512], in_=t1f[:, 512:1024],
                                     func=AF.Sin, scale=TWOPI)
                s_ms.inc(last); M("sincos_eval", s_ms)
                for qg in range(NQG):
                    for h in range(H):
                        for kt in range(TT):
                            par = kt % 2
                            if is_above(qg, kt):
                                sc.wait_ge(t_mm.h, V(f"sc_{qg}_{h}_{kt}"))
                                src = PS[:, par * 1024:par * 1024 + QG]
                            else:
                                sc.wait_ge(v_al.h, V(f"al_{qg}_{h}_{kt}"))
                                src = sten[:, par * QG:(par + 1) * QG]
                            if kt >= 2:
                                sc.wait_ge(t_av.h, V(f"av_{qg}_{h}_{kt-2}"))
                            elif not (qg == 0 and h == 0):
                                pq, ph = core_prev(qg, h)
                                sc.wait_ge(t_av.h, V(f"av_{pq}_{ph}_done"))
                            last = sc.activation(out=wex[:, par * QG:(par + 1) * QG],
                                                 in_=src, func=AF.Exp)
                            s_w.inc(last)
                            M(f"w_{qg}_{h}_{kt}", s_w)
                        sc.wait_ge(t_av.h, V(f"av_{qg}_{h}_done"))
                        if not (qg == 0 and h == 0):
                            pq, ph = core_prev(qg, h)
                            sc.wait_ge(d_at.h, V(f"atst_{pq}_{ph}"))
                        last = sc.copy(out=lrow[64:65, :], in_=PS[64:65, 2048:3072])
                        s_ev.inc(last)
                        M(f"lev_{qg}_{h}", s_ev)
            if do_ffn2:
                ln_s("ln3")
                ffn_s("ffn2")

    ctx.close()
    return nc


def _consts():
    import ml_dtypes
    bf = ml_dtypes.bfloat16
    ones_c = np.ones((1, 128), dtype=bf)
    zeros_c = np.zeros((1, 4096), dtype=bf)
    iota_f = np.arange(N, dtype=np.float32).reshape(1, N)
    iota_b = iota_f.astype(bf)
    iota_cl = np.arange(128, dtype=np.float32).reshape(128, 1)
    i = np.arange(128, dtype=np.float32).reshape(128, 1)
    c = np.arange(2 * N - 1, dtype=np.float32).reshape(1, 2 * N - 1)
    mst = 8.0 * np.minimum(i - c + (N - 1), 0.0)
    return dict(ones_c=ones_c, zeros_c=zeros_c, iota_f=iota_f, iota_b=iota_b,
                iota_cl=iota_cl, mst_c=np.ascontiguousarray(mst, dtype=np.float32))


_CACHE = {}


def _get_nc(upto="full"):
    if upto not in _CACHE:
        tape = Tape()
        build_program(tape, upto)
        tape.record = False
        _CACHE[upto] = build_program(tape, upto)
    return _CACHE[upto]


def kernel(L0_attn_x, L0_attn_qkv_w, L0_attn_proj_w, L0_attn_proj_b,
           L0_attn_slopes, L0_ffn_w_gate, L0_ffn_w_up, L0_ffn_w_down,
           _upto="full", _cores=8):
    nc = _get_nc(_upto)
    consts = _consts()
    base = dict(
        qkv_w=np.ascontiguousarray(L0_attn_qkv_w, dtype=np.float32),
        proj_w=np.ascontiguousarray(L0_attn_proj_w, dtype=np.float32),
        proj_b=np.ascontiguousarray(np.reshape(np.asarray(L0_attn_proj_b,
                                                          dtype=np.float32), (1, C))),
        slopes_b=np.ascontiguousarray(
            np.broadcast_to(np.reshape(np.asarray(L0_attn_slopes, dtype=np.float32),
                                       (1, H)), (128, H))),
        wg_i=np.ascontiguousarray(L0_ffn_w_gate, dtype=np.int32),
        wu_i=np.ascontiguousarray(L0_ffn_w_up, dtype=np.int32),
        wd_i=np.ascontiguousarray(L0_ffn_w_down, dtype=np.int32),
        **consts,
    )
    in_maps = []
    for b in range(_cores):
        m = dict(base)
        m["x"] = np.ascontiguousarray(L0_attn_x[b], dtype=np.float32)
        in_maps.append(m)
    res = run_bass_kernel_spmd(nc, in_maps, core_ids=list(range(_cores)))
    out = np.stack([np.asarray(r["y"]) for r in res.results], axis=0)
    if _cores < 8:
        pad = np.zeros((8 - _cores, N, C), dtype=np.float32)
        out = np.concatenate([out, pad], axis=0)
    return out


# version 02
# speedup vs baseline: 1.0612x; 1.0612x over previous
import math
from contextlib import ExitStack

import numpy as np
import concourse.bass as bass
import concourse.mybir as mybir
from concourse.bass_utils import run_bass_kernel_spmd

F32 = mybir.dt.float32
BF16 = mybir.dt.bfloat16
I32 = mybir.dt.int32
AF = mybir.ActivationFunctionType
OP = mybir.AluOpType
AX = mybir.AxisListType

N, C, H, DH, HID = 2048, 1024, 16, 64, 3072
TT, CCn = 16, 8
SCN = 4
SCH = HID // SCN
HFC = SCH // 128
NK = HFC * 4
NQG, QG = 2, 1024
S127 = 1.0 / 127.0
ALPHA = 0.5 * S127 * S127
EPS = 1e-5
INVSQRT8 = float(8.0 ** -0.5)
INV2PI = float(1.0 / (2.0 * math.pi))
TWOPI = float(2.0 * math.pi)

BASE = 16512
OF_X = BASE
OF_A = BASE + 65536
OF_K = BASE + 98304
OF_F = BASE + 131072
OF_W = BASE + 163840
PK = OF_X + 33280
SB_RESERVE = OF_W + 47360


class Sem:
    def __init__(self, nc, ctx, name):
        self.h = ctx.enter_context(nc.semaphore(name))
        self.n = 0
        self.name = name

    def inc(self, instr, k=1):
        instr.then_inc(self.h, k)
        self.n += k


class Tape:
    def __init__(self):
        self.d = {}
        self.record = True

    def mark(self, label, sem):
        if self.record:
            assert label not in self.d, f"dup label {label}"
            self.d[label] = sem.n
        else:
            assert self.d.get(label) == sem.n, (label, self.d.get(label), sem.n)

    def val(self, label):
        return 0 if self.record else self.d[label]


def build_program(tape, upto="full"):
    nc = bass.Bass("TRN2", target_bir_lowering=False, debug=False)

    x_in = nc.dram_tensor("x", [N, C], F32, kind="ExternalInput").ap()
    qkv_w = nc.dram_tensor("qkv_w", [3 * C, C], F32, kind="ExternalInput").ap()
    proj_w = nc.dram_tensor("proj_w", [C, C], F32, kind="ExternalInput").ap()
    proj_b = nc.dram_tensor("proj_b", [1, C], F32, kind="ExternalInput").ap()
    slopes_b = nc.dram_tensor("slopes_b", [128, H], F32, kind="ExternalInput").ap()
    wg_i = nc.dram_tensor("wg_i", [HID, C], I32, kind="ExternalInput").ap()
    wu_i = nc.dram_tensor("wu_i", [HID, C], I32, kind="ExternalInput").ap()
    wd_i = nc.dram_tensor("wd_i", [C, HID], I32, kind="ExternalInput").ap()
    ones_c = nc.dram_tensor("ones_c", [1, 128], BF16, kind="ExternalInput").ap()
    zeros_c = nc.dram_tensor("zeros_c", [1, 4096], BF16, kind="ExternalInput").ap()
    iota_f = nc.dram_tensor("iota_f", [1, N], F32, kind="ExternalInput").ap()
    iota_b = nc.dram_tensor("iota_b", [1, N], BF16, kind="ExternalInput").ap()
    iota_cl = nc.dram_tensor("iota_cl", [128, 1], F32, kind="ExternalInput").ap()
    mst_c = nc.dram_tensor("mst_c", [128, 2 * N - 1], F32, kind="ExternalInput").ap()
    y_out = nc.dram_tensor("y", [N, C], F32, kind="ExternalOutput").ap()

    wg_bf = nc.dram_tensor("wg_bf", [HID, C], BF16, kind="Internal").ap()
    wu_bf = nc.dram_tensor("wu_bf", [HID, C], BF16, kind="Internal").ap()
    wd_bf = nc.dram_tensor("wd_bf", [C, HID], BF16, kind="Internal").ap()
    qkv_bf = nc.dram_tensor("qkv_bf", [3 * C, C], BF16, kind="Internal").ap()
    proj_bf = nc.dram_tensor("proj_bf", [C, C], BF16, kind="Internal").ap()
    pb_bf = nc.dram_tensor("pb_bf", [1, C], BF16, kind="Internal").ap()
    ln_stage = nc.dram_tensor("ln_stage", [N, C], BF16, kind="Internal").ap()
    qk_stage = nc.dram_tensor("qk_stage", [N, 2 * H * 128], BF16, kind="Internal").ap()
    sc_stage = nc.dram_tensor("sc_stage", [128, 2048], BF16, kind="Internal").ap()
    fr_stage = nc.dram_tensor("fr_stage", [1, 32], F32, kind="Internal").ap()
    x_park = nc.dram_tensor("x_park", [128, TT * C], F32, kind="Internal").ap()
    at_stage = nc.dram_tensor("at_stage", [C, N], BF16, kind="Internal").ap()
    rl_stage = nc.dram_tensor("rl_stage", [H, N], F32, kind="Internal").ap()
    rlr_stage = nc.dram_tensor("rlr_stage", [H, N], F32, kind="Internal").ap()

    at = lambda name, shape, dtype, off: nc.alloc_sbuf_tensor_at(name, shape, dtype, offset=off)

    x_sb = at("x_sb", [128, TT * C], F32, OF_X)
    vP = at("vP", [128, TT * H * 65], BF16, OF_X)
    arena = at("arena", [128, 16384], BF16, OF_A)
    ktok = at("ktok", [128, 16384], BF16, OF_K)
    kall = at("kall", [67, H * N], BF16, OF_A)
    lnfm = at("lnfm", [128, CCn * N], BF16, OF_F)
    qall = at("qall", [67, H * N], BF16, OF_F)
    prin = at("prin", [128, CCn * N], BF16, OF_A)

    slsb = at("slsb", [128, H], F32, PK)
    atmp = at("atmp", [128, 2 * N], BF16, PK + 4416)
    iosb = at("iosb", [1, N], BF16, PK + 12608)
    sqb = at("sqb", [128, 1024], BF16, PK + 16704)
    nq2 = at("nq2", [128, TT * H], F32, PK + 18752)
    nk2 = at("nk2", [128, TT * H], F32, PK + 19776)
    smallf = at("smallf", [128, 192], F32, PK + 20800)
    smallb = at("smallb", [128, 512], BF16, PK + 21568)
    mnegall = at("mnegall", [128, 256], F32, PK + 22592)
    iscr = at("iscr", [128, 512], I32, PK + 23616)
    dtok = at("dtok", [128, 256], F32, PK + 25664)
    sten = at("sten", [128, 3 * QG], F32, PK + 4416)
    wex = at("wex", [128, 3 * QG], BF16, PK + 16704)
    asb = at("asb", [64, QG], BF16, PK + 22848)
    lrow = at("lrow", [65, QG], F32, PK + 24896)

    wgT = at("wgT", [128, CCn * SCH], BF16, OF_W)
    wuT = at("wuT", [128, CCn * SCH], BF16, OF_W + 12288)
    wdT = at("wdT", [128, HFC * C], BF16, OF_W + 24576)
    asil = at("asil", [128, 2 * 512], BF16, OF_W + 36864)
    bn6 = at("bn6", [128, TT * 12], F32, OF_W + 38912)
    mvb = at("mvb", [128, TT * 4], F32, OF_W + 39680)
    wqT = at("wqT", [128, CCn * C], BF16, OF_W)
    t1f = at("t1f", [128, 1024], F32, OF_W + 16384)
    t2f = at("t2f", [128, 1024], F32, OF_W + 20480)
    scc = at("scc", [128, 2048], BF16, OF_W + 24576)
    exs = at("exs", [128, 2 * C], BF16, OF_W + 28672)
    exc = at("exc", [128, 2 * C], BF16, OF_W + 32768)
    msp = at("msp", [128, N], BF16, OF_W + 36864)
    mrows = at("mrows", [128, N], BF16, OF_W + 40960)
    mstw = at("mstw", [128, 3072], F32, OF_W + 32768)
    prjT = at("prjT", [128, CCn * C], BF16, OF_W)
    praw = at("praw", [128, N], BF16, OF_W + 16384)
    rlb = at("rlb", [128, N], F32, OF_W + 20480)
    pbs = at("pbs", [1, C], BF16, OF_W + 28672)
    lsb = at("lsb", [H, N], F32, OF_W + 30720)
    rsb = at("rsb", [H, N], F32, OF_W + 38912)
    onesb = at("onesb", [1, 128], BF16, OF_W + 47104)

    assert OF_W + 47360 <= 229344 - 1024
    nc.sbuf_base = SB_RESERVE
    PS = nc.alloc_psum_tensor("PS", [128, 4096], F32)

    ctx = ExitStack()
    mk = lambda name: Sem(nc, ctx, name)
    d_x = mk("d_x"); d_c = mk("d_c"); d_ln = mk("d_ln"); d_w = mk("d_w"); d_wd = mk("d_wd")
    d_qk = mk("d_qk"); d_m = mk("d_m"); d_at = mk("d_at"); d_pj = mk("d_pj")
    d_out = mk("d_out"); d_pk = mk("d_pk"); d_fr = mk("d_fr"); d_kx = mk("d_kx"); d_st = mk("d_st")
    g_cast = mk("g_cast"); g_cast0 = mk("g_cast0"); g_cast2 = mk("g_cast2"); g_cast3 = mk("g_cast3")
    g_exp = mk("g_exp"); g_st = mk("g_st"); g_z = mk("g_z")
    t_mm = mk("t_mm"); t_av = mk("t_av")
    v_ms = mk("v_ms"); v_ev = mk("v_ev"); v_al = mk("v_al")
    s_ms = mk("s_ms"); s_ev = mk("s_ev"); s_w = mk("s_w")

    M = tape.mark
    V = tape.val

    do_attn = upto in ("attn", "attnonly", "full")
    do_ffn2 = upto == "full"
    do_ffn1 = upto in ("ffn1", "attn", "full")

    if upto == "ln1":
        sublayers = [("ln1", None)]
    elif upto == "ffn1":
        sublayers = [("ln1", "ffn1")]
    elif upto == "attn":
        sublayers = [("ln1", "ffn1"), ("ln2", None)]
    elif upto == "attnonly":
        sublayers = [("ln2", None)]
    else:
        sublayers = [("ln1", "ffn1"), ("ln2", None), ("ln3", "ffn2")]

    def is_above(qg, kt):
        return kt * 128 >= (qg + 1) * QG

    def core_prev(qg, h):
        return (qg, h - 1) if h > 0 else (qg - 1, H - 1)

    with nc.Block() as block:

        @block.sync
        def _(sy):
            i = sy.dma_start(out=x_sb[:].rearrange("p (t c) -> p t c", c=C),
                             in_=x_in.rearrange("(t p) c -> p t c", p=128))
            d_x.inc(i, 16)
            i = sy.dma_start(out=onesb[:], in_=ones_c); d_c.inc(i, 16)
            M("consts0", d_c)

            def ln_ffn_sync(lnname, ffnname):
                sy.wait_ge(v_ms.h, V(f"{lnname}_tok_done"))
                i = sy.dma_start(out=ln_stage.rearrange("(t p) c -> p t c", p=128),
                                 in_=arena[:].rearrange("p (t c) -> p t c", c=C))
                d_ln.inc(i, 16)
                sy.wait_ge(d_ln.h, d_ln.n)
                for cc in range(CCn):
                    i = sy.dma_start(out=lnfm[:, cc * N:(cc + 1) * N],
                                     in_=ln_stage[:, cc * 128:(cc + 1) * 128],
                                     transpose=True)
                    d_ln.inc(i, 16)
                M(f"{lnname}_fm_loaded", d_ln)

                if ffnname is not None:
                    sy.wait_ge(g_cast0.h, V("ffn_w_cast0"))
                    for sc in range(SCN):
                        if sc > 0:
                            sy.wait_ge(g_cast.h, V("ffn_w_cast"))
                            sy.wait_ge(t_mm.h, V(f"{ffnname}_gu{sc-1}_done"))
                        base = sc * SCH
                        for cc in range(CCn):
                            i = sy.dma_start(out=wgT[:, cc * SCH:(cc + 1) * SCH],
                                             in_=wg_bf[base:base + SCH, cc * 128:(cc + 1) * 128],
                                             transpose=True)
                            d_w.inc(i, 16)
                        for cc in range(CCn):
                            i = sy.dma_start(out=wuT[:, cc * SCH:(cc + 1) * SCH],
                                             in_=wu_bf[base:base + SCH, cc * 128:(cc + 1) * 128],
                                             transpose=True)
                            d_w.inc(i, 16)
                        M(f"{ffnname}_wgu{sc}", d_w)
                        if sc > 0:
                            sy.wait_ge(t_mm.h, V(f"{ffnname}_dn{sc-1}_done"))
                        for hf in range(HFC):
                            i = sy.dma_start(out=wdT[:, hf * C:(hf + 1) * C],
                                             in_=wd_bf[:, base + hf * 128:base + (hf + 1) * 128],
                                             transpose=True)
                            d_wd.inc(i, 16)
                        M(f"{ffnname}_w{sc}", d_wd)

            for (lnname, ffnname) in sublayers[:2]:
                ln_ffn_sync(lnname, ffnname)

            if do_attn:
                sy.wait_ge(v_ms.h, V("ln2_tok_done"))
                i = sy.dma_start(out=x_park, in_=x_sb[:]); d_pk.inc(i, 16)
                sy.wait_ge(d_pk.h, d_pk.n)
                i = sy.dma_start(out=slsb[:], in_=slopes_b); d_c.inc(i, 16)
                i = sy.dma_start(out=iosb[:], in_=iota_b); d_c.inc(i, 16)
                i = sy.dma_start(out=smallf[0:1, 0:32], in_=iota_f[0:1, 0:32]); d_c.inc(i, 16)
                i = sy.dma_start(out=smallf[:, 32:33], in_=iota_cl); d_c.inc(i, 16)
                M("prep_consts", d_c)
                sy.wait_ge(s_ms.h, V("fr_done"))
                i = sy.dma_start(out=fr_stage, in_=smallf[0:1, 64:96]); d_fr.inc(i, 16)
                sy.wait_ge(d_fr.h, d_fr.n)
                i = sy.dma_start(out=t2f[:, 0:512],
                                 in_=fr_stage[0:1, :].unsqueeze(1)
                                 .broadcast_to([1, 128 * 16, 32]))
                d_fr.inc(i, 16)
                M("frb_loaded", d_fr)

                sy.wait_ge(g_cast2.h, V("qkv_cast"))
                for gidx, gi in enumerate([2, 1, 0]):
                    if gidx > 0:
                        sy.wait_ge(t_mm.h, V(f"qkv_g{gidx-1}_done"))
                    for cc in range(CCn):
                        i = sy.dma_start(out=wqT[:, cc * C:(cc + 1) * C],
                                         in_=qkv_bf[gi * C:(gi + 1) * C, cc * 128:(cc + 1) * 128],
                                         transpose=True)
                        d_qk.inc(i, 16)
                    M(f"qkv_w{gidx}", d_qk)

                sy.wait_ge(d_qk.h, d_qk.n)
                sy.wait_ge(g_z.h, 16)
                sy.wait_ge(d_st.h, V("k_staged"))

                sy.wait_ge(v_ms.h, V("kmax0"))
                i = sy.dma_start(out=lsb[0:1, 0:2048], in_=smallf[:, 128:144])
                d_kx.inc(i, 16)
                sy.wait_ge(g_z.h, 16)
                for h in range(H):
                    if h >= 2:
                        sy.wait_ge(v_ms.h, V(f"kall_{h-2}_copied"))
                    i = sy.dma_start(out=atmp[:, (h % 2) * N:(h % 2 + 1) * N],
                                     in_=qk_stage[:, (H + h) * 128:(H + h + 1) * 128],
                                     transpose=True)
                    d_qk.inc(i, 16)
                    if h >= 1:
                        M(f"kT_{h-1}", d_qk)
                M(f"kT_{H-1}", d_qk)
                sy.wait_ge(v_ms.h, V(f"kall_{H-1}_copied"))
                sy.wait_ge(d_st.h, V("q_staged"))
                for h in range(H):
                    if h >= 2:
                        sy.wait_ge(v_ms.h, V(f"qall_{h-2}_copied"))
                    i = sy.dma_start(out=atmp[:, (h % 2) * N:(h % 2 + 1) * N],
                                     in_=qk_stage[:, h * 128:(h + 1) * 128],
                                     transpose=True)
                    d_qk.inc(i, 16)
                    if h >= 1:
                        M(f"qT_{h-1}", d_qk)
                M(f"qT_{H-1}", d_qk)
                sy.wait_ge(v_ms.h, V(f"qall_{H-1}_copied"))

                sy.wait_ge(v_ms.h, V("msp_done"))
                for tt in range(TT):
                    i = sy.dma_start(out=mrows[:, tt * 128:(tt + 1) * 128],
                                     in_=msp[:, tt * 128:(tt + 1) * 128], transpose=True)
                    d_m.inc(i, 16)
                sy.wait_ge(d_m.h, d_m.n)
                for h in range(H):
                    i = sy.dma_start(out=qall[64:67, h * N:(h + 1) * N],
                                     in_=mrows[8 * h:8 * h + 3, :])
                    d_m.inc(i, 16)
                M("qall_m_done", d_m)

                for qg in range(NQG):
                    if qg > 0:
                        sy.wait_ge(v_al.h, V(f"alibi_{qg-1}_done"))
                    else:
                        sy.wait_ge(d_m.h, d_m.n)
                        sy.wait_ge(d_c.h, d_c.n)
                    i = sy.dma_start(out=mstw[:, 0:3071], in_=mst_c[:, qg * QG:qg * QG + 3071])
                    d_c.inc(i, 16)
                    M(f"mstw_{qg}", d_c)
                    for h in range(H):
                        sy.wait_ge(v_ev.h, V(f"avev_{qg}_{h}"))
                        i = sy.dma_start(out=at_stage[h * 64:(h + 1) * 64, qg * QG:(qg + 1) * QG],
                                         in_=asb[:])
                        d_at.inc(i, 16)
                        sy.wait_ge(s_ev.h, V(f"lev_{qg}_{h}"))
                        i = sy.dma_start(out=rl_stage[h:h + 1, qg * QG:(qg + 1) * QG],
                                         in_=lrow[64:65, :])
                        d_at.inc(i, 16)
                        M(f"atst_{qg}_{h}", d_at)

                sy.wait_ge(d_at.h, d_at.n)
                i = sy.dma_start(out=lsb[:], in_=rl_stage); d_pj.inc(i, 16)
                M("lsb_loaded", d_pj)
                sy.wait_ge(v_ms.h, V("recip_done"))
                i = sy.dma_start(out=rlr_stage, in_=rsb[:]); d_pj.inc(i, 16)
                sy.wait_ge(d_pj.h, d_pj.n)
                sy.wait_ge(t_av.h, V("core_av_done"))
                i = sy.dma_start(out=x_sb[:], in_=x_park); d_pj.inc(i, 16)
                sy.wait_ge(g_cast3.h, V("proj_cast"))
                for cc in range(CCn):
                    i = sy.dma_start(out=prjT[:, cc * C:(cc + 1) * C],
                                     in_=proj_bf[:, cc * 128:(cc + 1) * 128], transpose=True)
                    d_pj.inc(i, 16)
                i = sy.dma_start(out=pbs[:], in_=pb_bf); d_pj.inc(i, 16)
                M("projw_loaded", d_pj)
                sy.wait_ge(d_pj.h, d_pj.n)
                for cc in range(CCn):
                    if cc >= 1:
                        sy.wait_ge(v_ms.h, V(f"prin_{cc-1}_done"))
                    i = sy.dma_start(out=praw[:], in_=at_stage[cc * 128:(cc + 1) * 128, :])
                    d_pj.inc(i, 16)
                    i = sy.dma_start(out=rlb[:],
                                     in_=rlr_stage[2 * cc:2 * cc + 2, :]
                                     .unsqueeze(1).broadcast_to([2, 64, N]))
                    d_pj.inc(i, 16)
                    M(f"prin_{cc}_loaded", d_pj)

            for (lnname, ffnname) in sublayers[2:]:
                ln_ffn_sync(lnname, ffnname)

            endsem, endmark = {
                "ln1": (v_ms, "ln1_out_ready"),
                "ffn1": (v_ev, "ffn1_x_done"),
                "attn": (v_ev, "proj_x_done"),
                "attnonly": (v_ev, "proj_x_done"),
                "full": (v_ev, "ffn2_x_done")}[upto]
            sy.wait_ge(endsem.h, V(endmark))
            i = sy.dma_start(out=y_out.rearrange("(t p) c -> p t c", p=128),
                             in_=x_sb[:].rearrange("p (t c) -> p t c", c=C))
            d_out.inc(i, 16)
            sy.wait_ge(d_out.h, d_out.n)

        @block.gpsimd
        def _(gp):
            if do_ffn1:
                i = gp.dma_start(out=wg_bf[0:SCH, :], in_=wg_i[0:SCH, :]); g_cast0.inc(i, 16)
                i = gp.dma_start(out=wu_bf[0:SCH, :], in_=wu_i[0:SCH, :]); g_cast0.inc(i, 16)
                i = gp.dma_start(out=wd_bf[:, 0:SCH], in_=wd_i[:, 0:SCH]); g_cast0.inc(i, 16)
                i = gp.dma_start(out=wg_bf[SCH:HID, :], in_=wg_i[SCH:HID, :]); g_cast.inc(i, 16)
                i = gp.dma_start(out=wu_bf[SCH:HID, :], in_=wu_i[SCH:HID, :]); g_cast.inc(i, 16)
                i = gp.dma_start(out=wd_bf[:, SCH:HID], in_=wd_i[:, SCH:HID]); g_cast.inc(i, 16)
            M("ffn_w_cast0", g_cast0)
            M("ffn_w_cast", g_cast)
            if do_attn:
                i = gp.dma_start(
                    out=qk_stage,
                    in_=zeros_c[0:1, :].broadcast_to([2048, 4096]))
                g_z.inc(i, 16)
                i = gp.dma_start(out=qkv_bf, in_=qkv_w); g_cast2.inc(i, 16)
                M("qkv_cast", g_cast2)
                i = gp.dma_start(out=proj_bf, in_=proj_w); g_cast3.inc(i, 16)
                i = gp.dma_start(out=pb_bf, in_=proj_b); g_cast3.inc(i, 16)
                M("proj_cast", g_cast3)
                gp.wait_ge(v_ms.h, V("scc_done"))
                i = gp.dma_start(out=sc_stage, in_=scc[:]); g_st.inc(i, 16)
                gp.wait_ge(g_st.h, g_st.n)
                for idx in range(2 * TT):
                    which, tt = divmod(idx, TT)
                    par = tt % 2
                    if idx >= 2:
                        pw, pt = divmod(idx - 2, TT)
                        gp.wait_ge(v_ms.h, V(f"rope{pw}_{pt}_used"))
                    i = gp.dma_start(
                        out=exs[:, par * C:(par + 1) * C],
                        in_=sc_stage[0:128, tt * 64:tt * 64 + 64]
                            .unsqueeze(1).broadcast_to([128, H, 64]))
                    g_exp.inc(i, 16)
                    i = gp.dma_start(
                        out=exc[:, par * C:(par + 1) * C],
                        in_=sc_stage[0:128, 1024 + tt * 64:1024 + tt * 64 + 64]
                            .unsqueeze(1).broadcast_to([128, H, 64]))
                    g_exp.inc(i, 16)
                    if idx >= 1:
                        pw, pt = divmod(idx - 1, TT)
                        M(f"exp{pw}_{pt}", g_exp)
                M(f"exp1_{TT-1}", g_exp)

        @block.tensor
        def _(te):
            def ffn_t(ffnname, lnname):
                te.wait_ge(d_ln.h, V(f"{lnname}_fm_loaded"))
                for sc in range(SCN):
                    te.wait_ge(d_w.h, V(f"{ffnname}_wgu{sc}"))
                    for hf in range(HFC):
                        for tg in range(4):
                            k = hf * 4 + tg
                            par = k % 2
                            gof, uof = par * 512, 1024 + par * 512
                            if k >= 2:
                                te.wait_ge(s_ms.h, V(f"{ffnname}_sil{sc}_{k-2}"))
                                te.wait_ge(v_ev.h, V(f"{ffnname}_h{sc}_{k-2}"))
                            elif sc > 0:
                                te.wait_ge(s_ms.h, V(f"{ffnname}_sil{sc-1}_{k+NK-2}"))
                                te.wait_ge(v_ev.h, V(f"{ffnname}_h{sc-1}_{k+NK-2}"))
                            last = None
                            for cc in range(CCn):
                                last = te.matmul(
                                    out=PS[:, gof:gof + 512],
                                    lhsT=wgT[:, cc * SCH + hf * 128:cc * SCH + (hf + 1) * 128],
                                    rhs=lnfm[:, cc * N + tg * 512:cc * N + (tg + 1) * 512],
                                    start=(cc == 0), stop=(cc == CCn - 1))
                            t_mm.inc(last)
                            M(f"{ffnname}_g{sc}_{k}", t_mm)
                            for cc in range(CCn):
                                last = te.matmul(
                                    out=PS[:, uof:uof + 512],
                                    lhsT=wuT[:, cc * SCH + hf * 128:cc * SCH + (hf + 1) * 128],
                                    rhs=lnfm[:, cc * N + tg * 512:cc * N + (tg + 1) * 512],
                                    start=(cc == 0), stop=(cc == CCn - 1))
                            t_mm.inc(last)
                            M(f"{ffnname}_u{sc}_{k}", t_mm)
                    M(f"{ffnname}_gu{sc}_done", t_mm)
                    te.wait_ge(d_wd.h, V(f"{ffnname}_w{sc}"))
                    for tt in range(TT):
                        par = tt % 2
                        dof = 2048 + par * 1024
                        if tt >= 2:
                            te.wait_ge(v_ev.h, V(f"{ffnname}_dnev{sc}_{tt-2}"))
                        elif sc > 0:
                            te.wait_ge(v_ev.h, V(f"{ffnname}_dnev{sc-1}_{tt+TT-2}"))
                        if tt == 0:
                            te.wait_ge(v_ev.h, V(f"{ffnname}_hsc{sc}_all"))
                        last = None
                        for hf in range(HFC):
                            for half in range(2):
                                last = te.matmul(
                                    out=PS[:, dof + half * 512:dof + (half + 1) * 512],
                                    lhsT=arena[:, hf * N + tt * 128:hf * N + (tt + 1) * 128],
                                    rhs=wdT[:, hf * C + half * 512:hf * C + (half + 1) * 512],
                                    start=(hf == 0), stop=(hf == HFC - 1))
                        t_mm.inc(last)
                        M(f"{ffnname}_dnmm{sc}_{tt}", t_mm)
                    M(f"{ffnname}_dn{sc}_done", t_mm)

            if do_ffn1:
                ffn_t("ffn1", "ln1")

            if do_attn:
                te.wait_ge(d_ln.h, V("ln2_fm_loaded"))
                for gidx in range(3):
                    te.wait_ge(d_qk.h, V(f"qkv_w{gidx}"))
                    for tt in range(TT):
                        par = tt % 2
                        pof = par * 1024
                        if gidx == 0 and tt < 2 and do_ffn1:
                            te.wait_ge(s_ms.h, V(f"ffn1_sil{SCN-1}_{NK-2+tt}"))
                            te.wait_ge(v_ev.h, V(f"ffn1_h{SCN-1}_{NK-2+tt}"))
                        if gidx == 0:
                            pass
                        if tt >= 2 or gidx > 0:
                            pidx = gidx * TT + tt - 2
                            pg, pt = divmod(pidx, TT)
                            te.wait_ge(v_ev.h, V(f"qkv_ev{pg}_{pt}"))
                        last = None
                        for half in range(2):
                            for cc in range(CCn):
                                last = te.matmul(
                                    out=PS[:, pof + half * 512:pof + (half + 1) * 512],
                                    lhsT=lnfm[:, cc * N + tt * 128:cc * N + (tt + 1) * 128],
                                    rhs=wqT[:, cc * C + half * 512:cc * C + (half + 1) * 512],
                                    start=(cc == 0), stop=(cc == CCn - 1))
                        t_mm.inc(last)
                        M(f"qkv_mm{gidx}_{tt}", t_mm)
                    M(f"qkv_g{gidx}_done", t_mm)

                te.wait_ge(v_ms.h, V("kk_ready"))
                te.wait_ge(d_c.h, V("prep_consts"))
                if do_ffn1:
                    te.wait_ge(v_ev.h, V(f"ffn1_dnev{SCN-1}_{TT-1}"))
                last = te.matmul(out=PS[:, 2048:2304], lhsT=onesb[:],
                                 rhs=smallb[0:1, 32:288], start=True, stop=True)
                for tt in range(TT):
                    last = te.matmul(out=PS[:, 2304 + tt * 16:2320 + tt * 16],
                                     lhsT=iosb[0:1, tt * 128:(tt + 1) * 128],
                                     rhs=smallb[0:1, 16:32], start=True, stop=True)
                t_mm.inc(last)
                M("mma_all", t_mm)

                te.wait_ge(d_m.h, V("qall_m_done"))
                te.wait_ge(v_ms.h, V(f"kall_{H-1}_copied"))
                te.wait_ge(v_ms.h, V("kall_ones"))
                te.wait_ge(v_ms.h, V(f"qall_{H-1}_copied"))
                for qg in range(NQG):
                    for h in range(H):
                        for kt in range(TT):
                            par = kt % 3
                            sof = par * 1024
                            if kt >= 3:
                                rq, rh, rkt = qg, h, kt - 3
                            elif not (qg == 0 and h == 0):
                                rq, rh = core_prev(qg, h)
                                rkt = kt + TT - 3
                            else:
                                rq = None
                            if rq is not None:
                                if is_above(rq, rkt):
                                    te.wait_ge(s_w.h, V(f"w_{rq}_{rh}_{rkt}"))
                                else:
                                    te.wait_ge(v_al.h, V(f"al_{rq}_{rh}_{rkt}"))
                            last = None
                            for half in range(2):
                                last = te.matmul(
                                    out=PS[:, sof + half * 512:sof + (half + 1) * 512],
                                    lhsT=kall[:, h * N + kt * 128:h * N + (kt + 1) * 128],
                                    rhs=qall[:, h * N + qg * QG + half * 512:
                                             h * N + qg * QG + (half + 1) * 512],
                                    start=True, stop=True)
                            t_mm.inc(last)
                            M(f"sc_{qg}_{h}_{kt}", t_mm)
                            if kt >= 1:
                                ktp = kt - 1
                                te.wait_ge(s_w.h, V(f"w_{qg}_{h}_{ktp}"))
                                if ktp == 0 and not (qg == 0 and h == 0):
                                    pq, ph = core_prev(qg, h)
                                    te.wait_ge(v_ev.h, V(f"avev_{pq}_{ph}"))
                                    te.wait_ge(s_ev.h, V(f"lev_{pq}_{ph}"))
                                wpar = ktp % 3
                                last = None
                                for half in range(2):
                                    last = te.matmul(
                                        out=PS[0:65, 3072 + half * 512:3072 + (half + 1) * 512],
                                        lhsT=vP[:, ktp * (H * 65) + h * 65:
                                                ktp * (H * 65) + (h + 1) * 65],
                                        rhs=wex[:, wpar * QG + half * 512:
                                                wpar * QG + (half + 1) * 512],
                                        start=(ktp == 0), stop=False)
                                t_av.inc(last)
                                M(f"av_{qg}_{h}_{ktp}", t_av)
                        ktp = TT - 1
                        te.wait_ge(s_w.h, V(f"w_{qg}_{h}_{ktp}"))
                        wpar = ktp % 3
                        last = None
                        for half in range(2):
                            last = te.matmul(
                                out=PS[0:65, 3072 + half * 512:3072 + (half + 1) * 512],
                                lhsT=vP[:, ktp * (H * 65) + h * 65:
                                        ktp * (H * 65) + (h + 1) * 65],
                                rhs=wex[:, wpar * QG + half * 512:
                                        wpar * QG + (half + 1) * 512],
                                start=False, stop=True)
                        t_av.inc(last)
                        M(f"av_{qg}_{h}_done", t_av)
                M("core_av_done", t_av)

                te.wait_ge(d_pj.h, V("projw_loaded"))
                te.wait_ge(v_ms.h, V(f"prin_{CCn-1}_done"))
                for tt in range(TT):
                    par = tt % 2
                    pof = par * 1024
                    if tt >= 2:
                        te.wait_ge(v_ev.h, V(f"projev_{tt-2}"))
                    last = None
                    for half in range(2):
                        for cc in range(CCn):
                            last = te.matmul(
                                out=PS[:, pof + half * 512:pof + (half + 1) * 512],
                                lhsT=prin[:, cc * N + tt * 128:cc * N + (tt + 1) * 128],
                                rhs=prjT[:, cc * C + half * 512:cc * C + (half + 1) * 512],
                                start=(cc == 0), stop=False)
                        last = te.matmul(
                            out=PS[:, pof + half * 512:pof + (half + 1) * 512],
                            lhsT=onesb[:],
                            rhs=pbs[0:1, half * 512:(half + 1) * 512],
                            start=False, stop=True)
                    t_mm.inc(last)
                    M(f"projmm_{tt}", t_mm)

            if do_ffn2:
                ffn_t("ffn2", "ln3")

        @block.vector
        def _(ve):
            def ln_v(lnname, waitsem, waitval):
                ve.wait_ge(waitsem.h, waitval)
                last = None
                for tt in range(TT):
                    for hlf in range(2):
                        last = ve.bn_stats(
                            out=bn6[:, tt * 12 + hlf * 6:tt * 12 + (hlf + 1) * 6],
                            in_=x_sb[:, tt * C + hlf * 512:tt * C + (hlf + 1) * 512])
                v_ms.inc(last); M(f"{lnname}_stats", v_ms)
                ve.wait_ge(v_ms.h, V(f"{lnname}_stats"))
                for tt in range(TT):
                    last = ve.bn_aggr(out=mvb[:, tt * 4:tt * 4 + 2],
                                      in_=bn6[:, tt * 12:(tt + 1) * 12])
                v_ms.inc(last); M(f"{lnname}_aggr", v_ms)
                ve.wait_ge(v_ms.h, V(f"{lnname}_aggr"))
                mv = mvb[:].rearrange("p (t f) -> p t f", f=4)
                last = ve.tensor_scalar(out=mv[:, :, 2:3], in0=mv[:, :, 1:2],
                                        scalar1=EPS, scalar2=None, op0=OP.add)
                v_ms.inc(last); M(f"{lnname}_veps", v_ms)
                ve.wait_ge(s_ms.h, V(f"{lnname}_sqrt"))
                last = ve.reciprocal(out=mv[:, :, 2:3], in_=mv[:, :, 3:4])
                v_ms.inc(last); M(f"{lnname}_rstd", v_ms)
                ve.wait_ge(v_ms.h, V(f"{lnname}_rstd"))
                for tt in range(TT):
                    last = ve.tensor_scalar(out=arena[:, tt * C:(tt + 1) * C],
                                            in0=x_sb[:, tt * C:(tt + 1) * C],
                                            scalar1=mvb[:, tt * 4:tt * 4 + 1],
                                            scalar2=mvb[:, tt * 4 + 2:tt * 4 + 3],
                                            op0=OP.subtract, op1=OP.mult)
                v_ms.inc(last); M(f"{lnname}_tok_done", v_ms)

            def ffn_v(ffnname):
                last = None
                for sc in range(SCN):
                    for hf in range(HFC):
                        for tg in range(4):
                            k = hf * 4 + tg
                            par = k % 2
                            uof = 1024 + par * 512
                            ve.wait_ge(s_ms.h, V(f"{ffnname}_sil{sc}_{k}"))
                            ve.wait_ge(t_mm.h, V(f"{ffnname}_u{sc}_{k}"))
                            if k == 0 and sc > 0:
                                ve.wait_ge(t_mm.h, V(f"{ffnname}_dn{sc-1}_done"))
                            last = ve.tensor_tensor(
                                out=arena[:, hf * N + tg * 512:hf * N + (tg + 1) * 512],
                                in0=asil[:, par * 512:(par + 1) * 512],
                                in1=PS[:, uof:uof + 512], op=OP.mult)
                            v_ev.inc(last)
                            M(f"{ffnname}_h{sc}_{k}", v_ev)
                    M(f"{ffnname}_hsc{sc}_all", v_ev)
                    for tt in range(TT):
                        par = tt % 2
                        dof = 2048 + par * 1024
                        ve.wait_ge(t_mm.h, V(f"{ffnname}_dnmm{sc}_{tt}"))
                        last = ve.scalar_tensor_tensor(
                            out=x_sb[:, tt * C:(tt + 1) * C],
                            in0=PS[:, dof:dof + 1024], scalar=ALPHA,
                            in1=x_sb[:, tt * C:(tt + 1) * C],
                            op0=OP.mult, op1=OP.add)
                        v_ev.inc(last)
                        M(f"{ffnname}_dnev{sc}_{tt}", v_ev)
                M(f"{ffnname}_x_done", v_ev)

            if upto != "attnonly":
                ln_v("ln1", d_x, 16)
            if upto == "ln1":
                ve.wait_ge(v_ms.h, V("ln1_tok_done"))
                last = None
                for tt in range(TT):
                    last = ve.tensor_copy(out=x_sb[:, tt * C:(tt + 1) * C],
                                          in_=arena[:, tt * C:(tt + 1) * C])
                v_ms.inc(last); M("ln1_out_ready", v_ms)
                return

            if do_ffn1:
                ffn_v("ffn1")
                ln_v("ln2", v_ev, V("ffn1_x_done"))
            elif upto == "attnonly":
                ln_v("ln2", d_x, 16)

            if do_attn:
                ve.wait_ge(d_c.h, V("prep_consts"))
                last = ve.tensor_scalar(out=smallf[0:1, 96:128], in0=smallf[0:1, 0:32],
                                        scalar1=float(-math.log(10000.0) / 32.0),
                                        scalar2=None, op0=OP.mult)
                v_ms.inc(last); M("frm_done", v_ms)
                ve.wait_ge(d_fr.h, V("frb_loaded"))
                for tt in range(TT):
                    last = ve.tensor_scalar(out=smallf[:, 33 + tt:34 + tt],
                                            in0=smallf[:, 32:33],
                                            scalar1=float(128 * tt), scalar2=INV2PI,
                                            op0=OP.add, op1=OP.mult)
                v_ms.inc(last); M("tv_done", v_ms)
                ve.wait_ge(v_ms.h, V("tv_done"))
                for tt in range(TT):
                    last = ve.tensor_scalar(out=t1f[:, tt * 32:(tt + 1) * 32],
                                            in0=t2f[:, tt * 32:(tt + 1) * 32],
                                            scalar1=smallf[:, 33 + tt:34 + tt],
                                            scalar2=None, op0=OP.mult)
                v_ms.inc(last); M("ang_done", v_ms)
                ve.wait_ge(v_ms.h, V("ang_done"))
                last = ve.tensor_scalar(out=iscr[:], in0=t1f[:, 0:512],
                                        scalar1=0.5, scalar2=None, op0=OP.add)
                v_ms.inc(last); M("rr1", v_ms)
                ve.wait_ge(v_ms.h, V("rr1"))
                last = ve.tensor_copy(out=t1f[:, 512:1024], in_=iscr[:])
                v_ms.inc(last); M("rr2", v_ms)
                ve.wait_ge(v_ms.h, V("rr2"))
                last = ve.tensor_sub(out=t2f[:, 512:1024], in0=t1f[:, 0:512],
                                     in1=t1f[:, 512:1024])
                v_ms.inc(last); M("rsin_done", v_ms)
                ve.wait_ge(v_ms.h, V("rsin_done"))
                last = ve.tensor_scalar(out=t1f[:, 0:512], in0=t1f[:, 0:512],
                                        scalar1=0.25, scalar2=None, op0=OP.add)
                v_ms.inc(last); M("rc0", v_ms)
                ve.wait_ge(v_ms.h, V("rc0"))
                last = ve.tensor_scalar(out=iscr[:], in0=t1f[:, 0:512],
                                        scalar1=0.5, scalar2=None, op0=OP.add)
                v_ms.inc(last); M("rc1", v_ms)
                ve.wait_ge(v_ms.h, V("rc1"))
                last = ve.tensor_copy(out=t1f[:, 512:1024], in_=iscr[:])
                v_ms.inc(last); M("rc2", v_ms)
                ve.wait_ge(v_ms.h, V("rc2"))
                last = ve.tensor_sub(out=t1f[:, 512:1024], in0=t1f[:, 0:512],
                                     in1=t1f[:, 512:1024])
                v_ms.inc(last); M("rcos_done", v_ms)
                ve.wait_ge(s_ms.h, V("sincos_eval"))
                sv = t1f[:, 0:512].rearrange("p (t j) -> p t j", j=32)
                cv = t2f[:, 0:512].rearrange("p (t j) -> p t j", j=32)
                sccv = scc[:].rearrange("p (k t j) -> p k t j", k=2, j=64)
                ve.tensor_scalar(out=sccv[:, 0:1, :, 0:32].squeeze(1), in0=sv,
                                 scalar1=-INVSQRT8, scalar2=None, op0=OP.mult)
                ve.tensor_scalar(out=sccv[:, 0:1, :, 32:64].squeeze(1), in0=sv,
                                 scalar1=INVSQRT8, scalar2=None, op0=OP.mult)
                ve.tensor_scalar(out=sccv[:, 1:2, :, 0:32].squeeze(1), in0=cv,
                                 scalar1=INVSQRT8, scalar2=None, op0=OP.mult)
                last = ve.tensor_scalar(out=sccv[:, 1:2, :, 32:64].squeeze(1), in0=cv,
                                        scalar1=INVSQRT8, scalar2=None, op0=OP.mult)
                v_ms.inc(last); M("scc_done", v_ms)

                ve.wait_ge(d_pk.h, 16)
                last = None
                for tt in range(TT):
                    par = tt % 2
                    pof = par * 1024
                    ve.wait_ge(t_mm.h, V(f"qkv_mm0_{tt}"))
                    vv = vP[:, tt * H * 65:(tt + 1) * H * 65].rearrange(
                        "p (h d) -> p h d", d=65)
                    ve.memset(vv[:, :, 64:65].squeeze(2), 1.0)
                    pv = PS[:, pof:pof + 1024].rearrange("p (h d) -> p h d", d=64)
                    last = ve.tensor_copy(out=vv[:, :, 0:64], in_=pv)
                    v_ev.inc(last)
                    M(f"qkv_ev0_{tt}", v_ev)

                for which, gsel, dst, n2 in ((0, 1, ktok, nk2), (1, 2, arena, nq2)):
                    if which == 1:
                        ve.wait_ge(d_ln.h, V("ln2_fm_loaded"))
                    for tt in range(TT):
                        par = tt % 2
                        pof = par * 1024
                        ve.wait_ge(t_mm.h, V(f"qkv_mm{gsel}_{tt}"))
                        ve.wait_ge(g_exp.h, V(f"exp{which}_{tt}"))
                        P = PS[:, pof:pof + 1024]
                        cosx = exc[:, par * C:(par + 1) * C]
                        sinx = exs[:, par * C:(par + 1) * C]
                        last = ve.tensor_tensor(out=t1f[:], in0=P, in1=cosx, op=OP.mult)
                        v_ms.inc(last); M(f"ropeA{which}_{tt}", v_ms)
                        Pv = P.rearrange("p (h d) -> p h d", d=64)
                        sxv = sinx.rearrange("p (h d) -> p h d", d=64)
                        t2v = t2f[:].rearrange("p (h d) -> p h d", d=64)
                        ve.tensor_tensor(out=t2v[:, :, 0:32], in0=Pv[:, :, 32:64],
                                         in1=sxv[:, :, 0:32], op=OP.mult)
                        last = ve.tensor_tensor(out=t2v[:, :, 32:64], in0=Pv[:, :, 0:32],
                                                in1=sxv[:, :, 32:64], op=OP.mult)
                        v_ms.inc(last)
                        M(f"rope{which}_{tt}_used", v_ms)
                        ve.wait_ge(v_ms.h, V(f"rope{which}_{tt}_used"))
                        last = ve.tensor_add(out=dst[:, tt * C:(tt + 1) * C],
                                             in0=t1f[:], in1=t2f[:])
                        v_ev.inc(last)
                        M(f"qkv_ev{gsel}_{tt}", v_ev)
                        ve.wait_ge(v_ev.h, V(f"qkv_ev{gsel}_{tt}"))
                        last = ve.tensor_mul(out=sqb[:], in0=dst[:, tt * C:(tt + 1) * C],
                                             in1=dst[:, tt * C:(tt + 1) * C])
                        v_ms.inc(last); M(f"sq{which}_{tt}", v_ms)
                        ve.wait_ge(v_ms.h, V(f"sq{which}_{tt}"))
                        last = ve.tensor_reduce(
                            out=n2[:, tt * H:(tt + 1) * H],
                            in_=sqb[:].rearrange("p (h d) -> p h d", d=64),
                            axis=AX.X, op=OP.add)
                        v_ms.inc(last); M(f"n2{which}_{tt}", v_ms)
                        if which == 1:
                            ve.wait_ge(v_ms.h, V(f"n2{which}_{tt}"))
                            last = ve.tensor_mul(out=sqb[:],
                                                 in0=arena[:, tt * C:(tt + 1) * C],
                                                 in1=ktok[:, tt * C:(tt + 1) * C])
                            v_ms.inc(last); M(f"sqd_{tt}", v_ms)
                            ve.wait_ge(v_ms.h, V(f"sqd_{tt}"))
                            last = ve.tensor_reduce(
                                out=dtok[:, tt * H:(tt + 1) * H],
                                in_=sqb[:].rearrange("p (h d) -> p h d", d=64),
                                axis=AX.X, op=OP.add, negate=True)
                            v_ms.inc(last); M(f"negd_{tt}", v_ms)
                    M("rope_k_done" if which == 0 else "rope_q_done", v_ms)

                last = ve.tensor_reduce(out=smallf[:, 128:144],
                                        in_=nk2[:].rearrange("p (t h) -> p h t", h=H),
                                        axis=AX.X, op=OP.max)
                v_ms.inc(last); M("kmax0", v_ms)
                ve.wait_ge(d_kx.h, 16)
                last = ve.tensor_reduce(
                    out=smallf[0:1, 128:144],
                    in_=lsb[0:1, 0:2048].rearrange("a (p h) -> a h p", h=16),
                    axis=AX.X, op=OP.max)
                v_ms.inc(last); M("kmax1", v_ms)
                ve.wait_ge(v_ms.h, V("kmax1"))
                last = ve.tensor_scalar(out=smallb[0:1, 32:48], in0=smallf[0:1, 128:144],
                                        scalar1=-0.5, scalar2=75.0, op0=OP.mult, op1=OP.add)
                v_ms.inc(last); M("kk1", v_ms)
                last = ve.tensor_scalar(out=smallb[0:1, 16:32], in0=slsb[0:1, :],
                                        scalar1=-8.0, scalar2=0.0, op0=OP.mult, op1=OP.max)
                v_ms.inc(last); M("kk2", v_ms)
                rep = 16
                while rep < 256:
                    ve.wait_ge(v_ms.h, v_ms.n)
                    last = ve.tensor_copy(out=smallb[0:1, 32 + rep:32 + 2 * rep],
                                          in_=smallb[0:1, 32:32 + rep])
                    v_ms.inc(last)
                    M(f"brow_{rep}", v_ms)
                    rep *= 2
                M("kk_ready", v_ms)

                ve.wait_ge(t_mm.h, V("mma_all"))
                last = ve.scalar_tensor_tensor(
                    out=mnegall[:], in0=nq2[:], scalar=-0.5,
                    in1=PS[:, 2048:2304], op0=OP.mult, op1=OP.add)
                v_ms.inc(last); M("mneg1", v_ms)
                ve.wait_ge(v_ms.h, V("mneg1"))
                last = ve.tensor_tensor(out=mnegall[:], in0=mnegall[:], in1=dtok[:],
                                        op=OP.min)
                v_ms.inc(last); M("mneg2", v_ms)
                ve.wait_ge(v_ms.h, V("mneg2"))
                last = ve.tensor_tensor(out=mnegall[:], in0=mnegall[:],
                                        in1=PS[:, 2304:2560], op=OP.subtract)
                v_ms.inc(last); M(f"mneg_{TT-1}", v_ms)
                mn = mnegall[:].rearrange("p (t h) -> p t h", h=H)
                mspv = msp[:].rearrange("p (t h s) -> p t h s", h=H, s=8)
                hi_b = mspv[:, :, :, 0:1].squeeze(3)
                mid_b = mspv[:, :, :, 1:2].squeeze(3)
                lo_b = mspv[:, :, :, 2:3].squeeze(3)
                hi_f = t1f[:, 0:256].rearrange("p (t h) -> p t h", h=H)
                d1 = t1f[:, 256:512].rearrange("p (t h) -> p t h", h=H)
                mid_f = t1f[:, 512:768].rearrange("p (t h) -> p t h", h=H)
                d2 = t1f[:, 768:1024].rearrange("p (t h) -> p t h", h=H)
                ve.wait_ge(v_ms.h, V(f"mneg_{TT-1}"))
                last = ve.tensor_copy(out=hi_b, in_=mn)
                v_ms.inc(last); M("mhi", v_ms)
                ve.wait_ge(v_ms.h, V("mhi"))
                last = ve.tensor_copy(out=hi_f, in_=hi_b)
                v_ms.inc(last); M("mhif", v_ms)
                ve.wait_ge(v_ms.h, V("mhif"))
                last = ve.tensor_sub(out=d1, in0=mn, in1=hi_f)
                v_ms.inc(last); M("md1", v_ms)
                ve.wait_ge(v_ms.h, V("md1"))
                last = ve.tensor_copy(out=mid_b, in_=d1)
                v_ms.inc(last); M("mmid", v_ms)
                ve.wait_ge(v_ms.h, V("mmid"))
                last = ve.tensor_copy(out=mid_f, in_=mid_b)
                v_ms.inc(last); M("mmidf", v_ms)
                ve.wait_ge(v_ms.h, V("mmidf"))
                last = ve.tensor_sub(out=d2, in0=d1, in1=mid_f)
                v_ms.inc(last); M("md2", v_ms)
                ve.wait_ge(v_ms.h, V("md2"))
                ve.tensor_copy(out=lo_b, in_=d2)
                last = ve.memset(mspv[:, :, :, 3:8], 0.0)
                v_ms.inc(last); M("msp_done", v_ms)

                ve.wait_ge(d_st.h, V("q_staged"))
                for h in range(H):
                    ve.wait_ge(d_qk.h, V(f"kT_{h}"))
                    last = ve.tensor_copy(out=kall[0:64, h * N:(h + 1) * N],
                                          in_=atmp[0:64, (h % 2) * N:(h % 2 + 1) * N])
                    v_ms.inc(last)
                    M(f"kall_{h}_copied", v_ms)
                last = ve.memset(kall[64:67, :], 1.0)
                v_ms.inc(last); M("kall_ones", v_ms)
                for h in range(H):
                    ve.wait_ge(d_qk.h, V(f"qT_{h}"))
                    last = ve.tensor_copy(out=qall[0:64, h * N:(h + 1) * N],
                                          in_=atmp[0:64, (h % 2) * N:(h % 2 + 1) * N])
                    v_ms.inc(last)
                    M(f"qall_{h}_copied", v_ms)

                for qg in range(NQG):
                    ve.wait_ge(d_c.h, V(f"mstw_{qg}"))
                    for h in range(H):
                        for kt in range(TT):
                            if is_above(qg, kt):
                                continue
                            par = kt % 3
                            sof = par * 1024
                            cs = max(0, kt * 128 - qg * QG)
                            ve.wait_ge(t_mm.h, V(f"sc_{qg}_{h}_{kt}"))
                            if kt >= 3:
                                ve.wait_ge(s_w.h, V(f"w_{qg}_{h}_{kt-3}"))
                            elif not (qg == 0 and h == 0):
                                pq, ph = core_prev(qg, h)
                                ve.wait_ge(s_w.h, V(f"w_{pq}_{ph}_{kt+TT-3}"))
                            cst = 2047 - kt * 128
                            last = ve.scalar_tensor_tensor(
                                out=sten[:, par * QG + cs:(par + 1) * QG],
                                in0=mstw[:, cst + cs:cst + QG],
                                scalar=slsb[:, h:h + 1],
                                in1=PS[:, sof + cs:sof + QG],
                                op0=OP.mult, op1=OP.add)
                            v_al.inc(last)
                            M(f"al_{qg}_{h}_{kt}", v_al)
                        ve.wait_ge(t_av.h, V(f"av_{qg}_{h}_done"))
                        if not (qg == 0 and h == 0):
                            pq, ph = core_prev(qg, h)
                            ve.wait_ge(d_at.h, V(f"atst_{pq}_{ph}"))
                        last = ve.tensor_copy(out=asb[:], in_=PS[0:64, 3072:4096])
                        v_ev.inc(last)
                        M(f"avev_{qg}_{h}", v_ev)
                    M(f"alibi_{qg}_done", v_al)

                ve.wait_ge(d_pj.h, V("lsb_loaded"))
                last = ve.reciprocal(out=rsb[:], in_=lsb[:])
                v_ms.inc(last); M("recip_done", v_ms)

                for cc in range(CCn):
                    ve.wait_ge(d_pj.h, V(f"prin_{cc}_loaded"))
                    last = ve.tensor_tensor(out=prin[:, cc * N:(cc + 1) * N],
                                            in0=praw[:], in1=rlb[:], op=OP.mult)
                    v_ms.inc(last)
                    M(f"prin_{cc}_done", v_ms)
                for tt in range(TT):
                    par = tt % 2
                    pof = par * 1024
                    ve.wait_ge(t_mm.h, V(f"projmm_{tt}"))
                    last = ve.tensor_tensor(out=x_sb[:, tt * C:(tt + 1) * C],
                                            in0=x_sb[:, tt * C:(tt + 1) * C],
                                            in1=PS[:, pof:pof + 1024], op=OP.add)
                    v_ev.inc(last)
                    M(f"projev_{tt}", v_ev)
                M("proj_x_done", v_ev)

            if do_ffn2:
                ln_v("ln3", v_ev, V("proj_x_done"))
                ffn_v("ffn2")

        @block.scalar
        def _(sc):
            def ln_s(lnname):
                sc.wait_ge(v_ms.h, V(f"{lnname}_veps"))
                mv = mvb[:].rearrange("p (t f) -> p t f", f=4)
                last = sc.activation(out=mv[:, :, 3:4], in_=mv[:, :, 2:3], func=AF.Sqrt)
                s_ms.inc(last)
                M(f"{lnname}_sqrt", s_ms)

            def ffn_s(ffnname):
                for scn in range(SCN):
                    for hf in range(HFC):
                        for tg in range(4):
                            k = hf * 4 + tg
                            par = k % 2
                            gof = par * 512
                            sc.wait_ge(t_mm.h, V(f"{ffnname}_g{scn}_{k}"))
                            if k >= 2:
                                sc.wait_ge(v_ev.h, V(f"{ffnname}_h{scn}_{k-2}"))
                            elif scn > 0:
                                sc.wait_ge(v_ev.h, V(f"{ffnname}_h{scn-1}_{k+NK-2}"))
                            last = sc.activation(out=asil[:, par * 512:(par + 1) * 512],
                                                 in_=PS[:, gof:gof + 512],
                                                 func=AF.Silu, scale=S127)
                            s_ms.inc(last)
                            M(f"{ffnname}_sil{scn}_{k}", s_ms)

            if upto != "attnonly":
                ln_s("ln1")
            if upto == "ln1":
                return
            if do_ffn1:
                ffn_s("ffn1")
                ln_s("ln2")
            elif upto == "attnonly":
                ln_s("ln2")
            if do_attn:
                sc.wait_ge(v_ms.h, V("frm_done"))
                last = sc.activation(out=smallf[0:1, 64:96], in_=smallf[0:1, 96:128],
                                     func=AF.Exp)
                s_ms.inc(last); M("fr_done", s_ms)
                sc.wait_ge(v_ms.h, V("rcos_done"))
                last = sc.activation(out=t1f[:, 0:512], in_=t2f[:, 512:1024],
                                     func=AF.Sin, scale=TWOPI)
                s_ms.inc(last); M("sin_eval", s_ms)
                last = sc.activation(out=t2f[:, 0:512], in_=t1f[:, 512:1024],
                                     func=AF.Sin, scale=TWOPI)
                s_ms.inc(last); M("sincos_eval", s_ms)
                qkv_view = qk_stage.rearrange("(t p) f -> p t f", p=128) \
                    .rearrange("p t (h c) -> p t h c", c=128)
                sc.wait_ge(g_z.h, 16)
                sc.wait_ge(v_ms.h, V("rope_k_done"))
                for tt in range(TT):
                    i = sc.dma_start(
                        out=qkv_view[:, tt:tt + 1, H:2 * H, 0:64].squeeze(1),
                        in_=ktok[:, tt * C:(tt + 1) * C]
                            .rearrange("p (h d) -> p h d", d=64))
                    d_st.inc(i, 16)
                M("k_staged", d_st)
                sc.wait_ge(d_st.h, d_st.n)
                sc.wait_ge(v_ms.h, V("rope_q_done"))
                for tt in range(TT):
                    i = sc.dma_start(
                        out=qkv_view[:, tt:tt + 1, 0:H, 0:64].squeeze(1),
                        in_=arena[:, tt * C:(tt + 1) * C]
                            .rearrange("p (h d) -> p h d", d=64))
                    d_st.inc(i, 16)
                M("q_staged", d_st)
                for qg in range(NQG):
                    for h in range(H):
                        for kt in range(TT):
                            par = kt % 3
                            cs = max(0, kt * 128 - qg * QG)
                            if kt >= 3:
                                sc.wait_ge(t_av.h, V(f"av_{qg}_{h}_{kt-3}"))
                            elif not (qg == 0 and h == 0):
                                pq, ph = core_prev(qg, h)
                                if kt + TT - 3 < TT - 1:
                                    sc.wait_ge(t_av.h, V(f"av_{pq}_{ph}_{kt+TT-3}"))
                                else:
                                    sc.wait_ge(t_av.h, V(f"av_{pq}_{ph}_done"))
                            if is_above(qg, kt):
                                sc.wait_ge(t_mm.h, V(f"sc_{qg}_{h}_{kt}"))
                                last = sc.activation(
                                    out=wex[:, par * QG:(par + 1) * QG],
                                    in_=PS[:, par * 1024:par * 1024 + QG], func=AF.Exp)
                            else:
                                sc.wait_ge(v_al.h, V(f"al_{qg}_{h}_{kt}"))
                                if cs > 0:
                                    sc.activation(
                                        out=wex[:, par * QG:par * QG + cs],
                                        in_=PS[:, par * 1024:par * 1024 + cs],
                                        func=AF.Exp)
                                last = sc.activation(
                                    out=wex[:, par * QG + cs:(par + 1) * QG],
                                    in_=sten[:, par * QG + cs:(par + 1) * QG],
                                    func=AF.Exp)
                            s_w.inc(last)
                            M(f"w_{qg}_{h}_{kt}", s_w)
                        sc.wait_ge(t_av.h, V(f"av_{qg}_{h}_done"))
                        if not (qg == 0 and h == 0):
                            pq, ph = core_prev(qg, h)
                            sc.wait_ge(d_at.h, V(f"atst_{pq}_{ph}"))
                        last = sc.copy(out=lrow[64:65, :], in_=PS[64:65, 3072:4096])
                        s_ev.inc(last)
                        M(f"lev_{qg}_{h}", s_ev)
            if do_ffn2:
                ln_s("ln3")
                ffn_s("ffn2")

    ctx.close()
    return nc


def _consts():
    import ml_dtypes
    bf = ml_dtypes.bfloat16
    ones_c = np.ones((1, 128), dtype=bf)
    zeros_c = np.zeros((1, 4096), dtype=bf)
    iota_f = np.arange(N, dtype=np.float32).reshape(1, N)
    iota_b = iota_f.astype(bf)
    iota_cl = np.arange(128, dtype=np.float32).reshape(128, 1)
    i = np.arange(128, dtype=np.float32).reshape(128, 1)
    c = np.arange(2 * N - 1, dtype=np.float32).reshape(1, 2 * N - 1)
    mst = 8.0 * np.minimum(i - c + (N - 1), 0.0)
    return dict(ones_c=ones_c, zeros_c=zeros_c, iota_f=iota_f, iota_b=iota_b,
                iota_cl=iota_cl, mst_c=np.ascontiguousarray(mst, dtype=np.float32))


_CACHE = {}


def _get_nc(upto="full"):
    if upto not in _CACHE:
        tape = Tape()
        build_program(tape, upto)
        tape.record = False
        _CACHE[upto] = build_program(tape, upto)
    return _CACHE[upto]


def kernel(L0_attn_x, L0_attn_qkv_w, L0_attn_proj_w, L0_attn_proj_b,
           L0_attn_slopes, L0_ffn_w_gate, L0_ffn_w_up, L0_ffn_w_down,
           _upto="full", _cores=8):
    nc = _get_nc(_upto)
    consts = _consts()
    base = dict(
        qkv_w=np.ascontiguousarray(L0_attn_qkv_w, dtype=np.float32),
        proj_w=np.ascontiguousarray(L0_attn_proj_w, dtype=np.float32),
        proj_b=np.ascontiguousarray(np.reshape(np.asarray(L0_attn_proj_b,
                                                          dtype=np.float32), (1, C))),
        slopes_b=np.ascontiguousarray(
            np.broadcast_to(np.reshape(np.asarray(L0_attn_slopes, dtype=np.float32),
                                       (1, H)), (128, H))),
        wg_i=np.ascontiguousarray(L0_ffn_w_gate, dtype=np.int32),
        wu_i=np.ascontiguousarray(L0_ffn_w_up, dtype=np.int32),
        wd_i=np.ascontiguousarray(L0_ffn_w_down, dtype=np.int32),
        **consts,
    )
    in_maps = []
    for b in range(_cores):
        m = dict(base)
        m["x"] = np.ascontiguousarray(L0_attn_x[b], dtype=np.float32)
        in_maps.append(m)
    res = run_bass_kernel_spmd(nc, in_maps, core_ids=list(range(_cores)))
    out = np.stack([np.asarray(r["y"]) for r in res.results], axis=0)
    if _cores < 8:
        pad = np.zeros((8 - _cores, N, C), dtype=np.float32)
        out = np.concatenate([out, pad], axis=0)
    return out


# version 03
# speedup vs baseline: 1.0783x; 1.0161x over previous
import math
from contextlib import ExitStack

import numpy as np
import concourse.bass as bass
import concourse.mybir as mybir
from concourse.bass_utils import run_bass_kernel_spmd

F32 = mybir.dt.float32
BF16 = mybir.dt.bfloat16
I32 = mybir.dt.int32
AF = mybir.ActivationFunctionType
OP = mybir.AluOpType
AX = mybir.AxisListType

N, C, H, DH, HID = 2048, 1024, 16, 64, 3072
TT, CCn = 16, 8
SCN = 4
SCH = HID // SCN
HFC = SCH // 128
NK = HFC * 4
NQG, QG = 2, 1024
S127 = 1.0 / 127.0
ALPHA = 0.5 * S127 * S127
EPS = 1e-5
INVSQRT8 = float(8.0 ** -0.5)
INV2PI = float(1.0 / (2.0 * math.pi))
TWOPI = float(2.0 * math.pi)

BASE = 16512
OF_X = BASE
OF_A = BASE + 65536
OF_K = BASE + 98304
OF_F = BASE + 131072
OF_W = BASE + 163840
PK = OF_X + 33280
SB_RESERVE = OF_W + 47360


class Sem:
    def __init__(self, nc, ctx, name):
        self.h = ctx.enter_context(nc.semaphore(name))
        self.n = 0
        self.name = name

    def inc(self, instr, k=1):
        instr.then_inc(self.h, k)
        self.n += k


class Tape:
    def __init__(self):
        self.d = {}
        self.record = True

    def mark(self, label, sem):
        if self.record:
            assert label not in self.d, f"dup label {label}"
            self.d[label] = sem.n
        else:
            assert self.d.get(label) == sem.n, (label, self.d.get(label), sem.n)

    def val(self, label):
        return 0 if self.record else self.d[label]


def build_program(tape, upto="full"):
    nc = bass.Bass("TRN2", target_bir_lowering=False, debug=False)

    x_in = nc.dram_tensor("x", [N, C], F32, kind="ExternalInput").ap()
    qkv_w = nc.dram_tensor("qkv_w", [3 * C, C], F32, kind="ExternalInput").ap()
    proj_w = nc.dram_tensor("proj_w", [C, C], F32, kind="ExternalInput").ap()
    proj_b = nc.dram_tensor("proj_b", [1, C], F32, kind="ExternalInput").ap()
    slopes_b = nc.dram_tensor("slopes_b", [128, H], F32, kind="ExternalInput").ap()
    wg_i = nc.dram_tensor("wg_i", [HID, C], I32, kind="ExternalInput").ap()
    wu_i = nc.dram_tensor("wu_i", [HID, C], I32, kind="ExternalInput").ap()
    wd_i = nc.dram_tensor("wd_i", [C, HID], I32, kind="ExternalInput").ap()
    ones_c = nc.dram_tensor("ones_c", [1, 128], BF16, kind="ExternalInput").ap()
    zeros_c = nc.dram_tensor("zeros_c", [1, 4096], BF16, kind="ExternalInput").ap()
    iota_f = nc.dram_tensor("iota_f", [1, N], F32, kind="ExternalInput").ap()
    iota_b = nc.dram_tensor("iota_b", [1, N], BF16, kind="ExternalInput").ap()
    iota_cl = nc.dram_tensor("iota_cl", [128, 1], F32, kind="ExternalInput").ap()
    mst_c = nc.dram_tensor("mst_c", [128, 2 * N - 1], F32, kind="ExternalInput").ap()
    y_out = nc.dram_tensor("y", [N, C], F32, kind="ExternalOutput").ap()

    wg_bf = nc.dram_tensor("wg_bf", [HID, C], BF16, kind="Internal").ap()
    wu_bf = nc.dram_tensor("wu_bf", [HID, C], BF16, kind="Internal").ap()
    wd_bf = nc.dram_tensor("wd_bf", [C, HID], BF16, kind="Internal").ap()
    qkv_bf = nc.dram_tensor("qkv_bf", [3 * C, C], BF16, kind="Internal").ap()
    proj_bf = nc.dram_tensor("proj_bf", [C, C], BF16, kind="Internal").ap()
    pb_bf = nc.dram_tensor("pb_bf", [1, C], BF16, kind="Internal").ap()
    ln_stage = nc.dram_tensor("ln_stage", [N, C], BF16, kind="Internal").ap()
    qk_stage = nc.dram_tensor("qk_stage", [N, 2 * H * 128], BF16, kind="Internal").ap()
    sc_stage = nc.dram_tensor("sc_stage", [128, 2048], BF16, kind="Internal").ap()
    fr_stage = nc.dram_tensor("fr_stage", [1, 32], F32, kind="Internal").ap()
    x_park = nc.dram_tensor("x_park", [128, TT * C], F32, kind="Internal").ap()
    at_stage = nc.dram_tensor("at_stage", [C, N], BF16, kind="Internal").ap()
    rl_stage = nc.dram_tensor("rl_stage", [H, N], F32, kind="Internal").ap()
    rlr_stage = nc.dram_tensor("rlr_stage", [H, N], F32, kind="Internal").ap()

    at = lambda name, shape, dtype, off: nc.alloc_sbuf_tensor_at(name, shape, dtype, offset=off)

    x_sb = at("x_sb", [128, TT * C], F32, OF_X)
    vP = at("vP", [128, TT * H * 65], BF16, OF_X)
    arena = at("arena", [128, 16384], BF16, OF_A)
    ktok = at("ktok", [128, 16384], BF16, OF_K)
    kall = at("kall", [67, H * N], BF16, OF_A)
    lnfm = at("lnfm", [128, CCn * N], BF16, OF_F)
    qall = at("qall", [67, H * N], BF16, OF_F)
    prin = at("prin", [128, CCn * N], BF16, OF_A)

    slsb = at("slsb", [128, H], F32, PK)
    atmp = at("atmp", [128, 2 * N], BF16, PK + 4416)
    iosb = at("iosb", [1, N], BF16, PK + 12608)
    sqb = at("sqb", [128, 1024], BF16, PK + 16704)
    nq2 = at("nq2", [128, TT * H], F32, PK + 18752)
    nk2 = at("nk2", [128, TT * H], F32, PK + 19776)
    smallf = at("smallf", [128, 192], F32, PK + 20800)
    smallb = at("smallb", [128, 512], BF16, PK + 21568)
    mnegall = at("mnegall", [128, 256], F32, PK + 22592)
    iscr = at("iscr", [128, 512], I32, PK + 23616)
    dtok = at("dtok", [128, 256], F32, PK + 25664)
    sten = at("sten", [128, 3 * QG], F32, PK + 4416)
    wex = at("wex", [128, 3 * QG], BF16, PK + 16704)
    asb = at("asb", [64, QG], BF16, PK + 22848)
    lrow = at("lrow", [65, QG], F32, PK + 24896)

    wgT = at("wgT", [128, CCn * SCH], BF16, OF_W)
    wuT = at("wuT", [128, CCn * SCH], BF16, OF_W + 12288)
    wdT = at("wdT", [128, HFC * C], BF16, OF_W + 24576)
    asil = at("asil", [128, 2 * 512], BF16, OF_W + 36864)
    bn6 = at("bn6", [128, TT * 12], F32, OF_W + 38912)
    mvb = at("mvb", [128, TT * 4], F32, OF_W + 39680)
    wqT = at("wqT", [128, CCn * C], BF16, OF_W)
    t1f = at("t1f", [128, 1024], F32, OF_W + 16384)
    t2f = at("t2f", [128, 1024], F32, OF_W + 20480)
    scc = at("scc", [128, 2048], BF16, OF_W + 24576)
    exs = at("exs", [128, 2 * C], BF16, OF_W + 28672)
    exc = at("exc", [128, 2 * C], BF16, OF_W + 32768)
    msp = at("msp", [128, N], BF16, OF_W + 36864)
    mrows = at("mrows", [128, N], BF16, OF_W + 40960)
    mstw = at("mstw", [128, 3072], F32, OF_W + 32768)
    prjT = at("prjT", [128, CCn * C], BF16, OF_W)
    praw = at("praw", [128, N], BF16, OF_W + 16384)
    rlb = at("rlb", [128, N], F32, OF_W + 20480)
    pbs = at("pbs", [1, C], BF16, OF_W + 28672)
    lsb = at("lsb", [H, N], F32, OF_W + 30720)
    rsb = at("rsb", [H, N], F32, OF_W + 38912)
    onesb = at("onesb", [1, 128], BF16, OF_W + 47104)

    assert OF_W + 47360 <= 229344 - 1024
    nc.sbuf_base = SB_RESERVE
    PS = nc.alloc_psum_tensor("PS", [128, 4096], F32)

    ctx = ExitStack()
    mk = lambda name: Sem(nc, ctx, name)
    d_x = mk("d_x"); d_c = mk("d_c"); d_ln = mk("d_ln"); d_w = mk("d_w"); d_wd = mk("d_wd")
    d_qk = mk("d_qk"); d_m = mk("d_m"); d_at = mk("d_at"); d_pj = mk("d_pj")
    d_out = mk("d_out"); d_pk = mk("d_pk"); d_fr = mk("d_fr"); d_kx = mk("d_kx"); d_st = mk("d_st")
    g_cast = mk("g_cast"); g_cast0 = mk("g_cast0"); g_cast2 = mk("g_cast2"); g_cast3 = mk("g_cast3")
    g_exp = mk("g_exp"); g_st = mk("g_st"); g_z = mk("g_z")
    t_mm = mk("t_mm"); t_av = mk("t_av")
    v_ms = mk("v_ms"); v_ev = mk("v_ev"); v_al = mk("v_al")
    s_ms = mk("s_ms"); s_ev = mk("s_ev"); s_w = mk("s_w")

    M = tape.mark
    V = tape.val

    do_attn = upto in ("attn", "attnonly", "full")
    do_ffn2 = upto == "full"
    do_ffn1 = upto in ("ffn1", "attn", "full")

    if upto == "ln1":
        sublayers = [("ln1", None)]
    elif upto == "ffn1":
        sublayers = [("ln1", "ffn1")]
    elif upto == "attn":
        sublayers = [("ln1", "ffn1"), ("ln2", None)]
    elif upto == "attnonly":
        sublayers = [("ln2", None)]
    else:
        sublayers = [("ln1", "ffn1"), ("ln2", None), ("ln3", "ffn2")]

    def is_above(qg, kt):
        return kt * 128 >= (qg + 1) * QG

    def core_prev(qg, h):
        return (qg, h - 1) if h > 0 else (qg - 1, H - 1)

    with nc.Block() as block:

        @block.sync
        def _(sy):
            i = sy.dma_start(out=x_sb[:].rearrange("p (t c) -> p t c", c=C),
                             in_=x_in.rearrange("(t p) c -> p t c", p=128))
            d_x.inc(i, 16)
            i = sy.dma_start(out=onesb[:], in_=ones_c); d_c.inc(i, 16)
            M("consts0", d_c)

            def ln_ffn_sync(lnname, ffnname):
                sy.wait_ge(v_ms.h, V(f"{lnname}_tok_done"))
                i = sy.dma_start(out=ln_stage.rearrange("(t p) c -> p t c", p=128),
                                 in_=arena[:].rearrange("p (t c) -> p t c", c=C))
                d_ln.inc(i, 16)
                sy.wait_ge(d_ln.h, d_ln.n)
                for cc in range(CCn):
                    i = sy.dma_start(out=lnfm[:, cc * N:(cc + 1) * N],
                                     in_=ln_stage[:, cc * 128:(cc + 1) * 128],
                                     transpose=True)
                    d_ln.inc(i, 16)
                M(f"{lnname}_fm_loaded", d_ln)

                if ffnname is not None:
                    sy.wait_ge(g_cast0.h, V("ffn_w_cast0"))
                    for sc in range(SCN):
                        if sc > 0:
                            sy.wait_ge(g_cast.h, V("ffn_w_cast"))
                            sy.wait_ge(t_mm.h, V(f"{ffnname}_gu{sc-1}_done"))
                        base = sc * SCH
                        for cc in range(CCn):
                            i = sy.dma_start(out=wgT[:, cc * SCH:(cc + 1) * SCH],
                                             in_=wg_bf[base:base + SCH, cc * 128:(cc + 1) * 128],
                                             transpose=True)
                            d_w.inc(i, 16)
                        for cc in range(CCn):
                            i = sy.dma_start(out=wuT[:, cc * SCH:(cc + 1) * SCH],
                                             in_=wu_bf[base:base + SCH, cc * 128:(cc + 1) * 128],
                                             transpose=True)
                            d_w.inc(i, 16)
                        M(f"{ffnname}_wgu{sc}", d_w)
                        if sc > 0:
                            sy.wait_ge(t_mm.h, V(f"{ffnname}_dn{sc-1}_done"))
                        for hf in range(HFC):
                            i = sy.dma_start(out=wdT[:, hf * C:(hf + 1) * C],
                                             in_=wd_bf[:, base + hf * 128:base + (hf + 1) * 128],
                                             transpose=True)
                            d_wd.inc(i, 16)
                        M(f"{ffnname}_w{sc}", d_wd)

            for (lnname, ffnname) in sublayers[:2]:
                ln_ffn_sync(lnname, ffnname)

            if do_attn:
                sy.wait_ge(v_ms.h, V("ln2_tok_done"))
                i = sy.dma_start(out=x_park, in_=x_sb[:]); d_pk.inc(i, 16)
                sy.wait_ge(d_pk.h, d_pk.n)
                i = sy.dma_start(out=slsb[:], in_=slopes_b); d_c.inc(i, 16)
                i = sy.dma_start(out=iosb[:], in_=iota_b); d_c.inc(i, 16)
                i = sy.dma_start(out=smallf[0:1, 0:32], in_=iota_f[0:1, 0:32]); d_c.inc(i, 16)
                i = sy.dma_start(out=smallf[:, 32:33], in_=iota_cl); d_c.inc(i, 16)
                M("prep_consts", d_c)
                sy.wait_ge(s_ms.h, V("fr_done"))
                i = sy.dma_start(out=fr_stage, in_=smallf[0:1, 64:96]); d_fr.inc(i, 16)
                sy.wait_ge(d_fr.h, d_fr.n)
                i = sy.dma_start(out=t2f[:, 0:512],
                                 in_=fr_stage[0:1, :].unsqueeze(1)
                                 .broadcast_to([1, 128 * 16, 32]))
                d_fr.inc(i, 16)
                M("frb_loaded", d_fr)

                sy.wait_ge(g_cast2.h, V("qkv_cast"))
                for gidx, gi in enumerate([2, 1, 0]):
                    if gidx > 0:
                        sy.wait_ge(t_mm.h, V(f"qkv_g{gidx-1}_done"))
                    for cc in range(CCn):
                        i = sy.dma_start(out=wqT[:, cc * C:(cc + 1) * C],
                                         in_=qkv_bf[gi * C:(gi + 1) * C, cc * 128:(cc + 1) * 128],
                                         transpose=True)
                        d_qk.inc(i, 16)
                    M(f"qkv_w{gidx}", d_qk)

                sy.wait_ge(d_qk.h, d_qk.n)
                sy.wait_ge(g_z.h, 16)
                sy.wait_ge(d_st.h, V("k_staged"))

                sy.wait_ge(v_ms.h, V("kmax0"))
                i = sy.dma_start(out=lsb[0:1, 0:2048], in_=smallf[:, 128:144])
                d_kx.inc(i, 16)
                sy.wait_ge(g_z.h, 16)
                for h in range(H):
                    if h >= 2:
                        sy.wait_ge(v_ms.h, V(f"kall_{h-2}_copied"))
                    i = sy.dma_start(out=atmp[:, (h % 2) * N:(h % 2 + 1) * N],
                                     in_=qk_stage[:, (H + h) * 128:(H + h + 1) * 128],
                                     transpose=True)
                    d_qk.inc(i, 16)
                    if h >= 1:
                        M(f"kT_{h-1}", d_qk)
                M(f"kT_{H-1}", d_qk)
                sy.wait_ge(v_ms.h, V(f"kall_{H-1}_copied"))
                sy.wait_ge(d_st.h, V("q_staged"))
                for h in range(H):
                    if h >= 2:
                        sy.wait_ge(v_ms.h, V(f"qall_{h-2}_copied"))
                    i = sy.dma_start(out=atmp[:, (h % 2) * N:(h % 2 + 1) * N],
                                     in_=qk_stage[:, h * 128:(h + 1) * 128],
                                     transpose=True)
                    d_qk.inc(i, 16)
                    if h >= 1:
                        M(f"qT_{h-1}", d_qk)
                M(f"qT_{H-1}", d_qk)
                sy.wait_ge(v_ms.h, V(f"qall_{H-1}_copied"))

                sy.wait_ge(v_ms.h, V("msp_done"))
                for tt in range(TT):
                    i = sy.dma_start(out=mrows[:, tt * 128:(tt + 1) * 128],
                                     in_=msp[:, tt * 128:(tt + 1) * 128], transpose=True)
                    d_m.inc(i, 16)
                sy.wait_ge(d_m.h, d_m.n)
                for h in range(H):
                    i = sy.dma_start(out=qall[64:67, h * N:(h + 1) * N],
                                     in_=mrows[8 * h:8 * h + 3, :])
                    d_m.inc(i, 16)
                M("qall_m_done", d_m)

                for qg in range(NQG):
                    if qg > 0:
                        sy.wait_ge(v_al.h, V(f"alibi_{qg-1}_done"))
                    else:
                        sy.wait_ge(d_m.h, d_m.n)
                        sy.wait_ge(d_c.h, d_c.n)
                    i = sy.dma_start(out=mstw[:, 0:3071], in_=mst_c[:, qg * QG:qg * QG + 3071])
                    d_c.inc(i, 16)
                    M(f"mstw_{qg}", d_c)
                    for h in range(H):
                        sy.wait_ge(v_ev.h, V(f"avev_{qg}_{h}"))
                        i = sy.dma_start(out=at_stage[h * 64:(h + 1) * 64, qg * QG:(qg + 1) * QG],
                                         in_=asb[:])
                        d_at.inc(i, 16)
                        sy.wait_ge(s_ev.h, V(f"lev_{qg}_{h}"))
                        i = sy.dma_start(out=rl_stage[h:h + 1, qg * QG:(qg + 1) * QG],
                                         in_=lrow[64:65, :])
                        d_at.inc(i, 16)
                        M(f"atst_{qg}_{h}", d_at)

                sy.wait_ge(d_at.h, d_at.n)
                i = sy.dma_start(out=lsb[:], in_=rl_stage); d_pj.inc(i, 16)
                M("lsb_loaded", d_pj)
                sy.wait_ge(v_ms.h, V("recip_done"))
                i = sy.dma_start(out=rlr_stage, in_=rsb[:]); d_pj.inc(i, 16)
                sy.wait_ge(d_pj.h, d_pj.n)
                sy.wait_ge(t_av.h, V("core_av_done"))
                i = sy.dma_start(out=x_sb[:], in_=x_park); d_pj.inc(i, 16)
                sy.wait_ge(g_cast3.h, V("proj_cast"))
                for cc in range(CCn):
                    i = sy.dma_start(out=prjT[:, cc * C:(cc + 1) * C],
                                     in_=proj_bf[:, cc * 128:(cc + 1) * 128], transpose=True)
                    d_pj.inc(i, 16)
                i = sy.dma_start(out=pbs[:], in_=pb_bf); d_pj.inc(i, 16)
                M("projw_loaded", d_pj)
                sy.wait_ge(d_pj.h, d_pj.n)
                for cc in range(CCn):
                    if cc >= 1:
                        sy.wait_ge(v_ms.h, V(f"prin_{cc-1}_done"))
                    i = sy.dma_start(out=praw[:], in_=at_stage[cc * 128:(cc + 1) * 128, :])
                    d_pj.inc(i, 16)
                    i = sy.dma_start(out=rlb[:],
                                     in_=rlr_stage[2 * cc:2 * cc + 2, :]
                                     .unsqueeze(1).broadcast_to([2, 64, N]))
                    d_pj.inc(i, 16)
                    M(f"prin_{cc}_loaded", d_pj)

            for (lnname, ffnname) in sublayers[2:]:
                ln_ffn_sync(lnname, ffnname)

            endsem, endmark = {
                "ln1": (v_ms, "ln1_out_ready"),
                "ffn1": (v_ev, "ffn1_x_done"),
                "attn": (v_ev, "proj_x_done"),
                "attnonly": (v_ev, "proj_x_done"),
                "full": (v_ev, "ffn2_x_done")}[upto]
            sy.wait_ge(endsem.h, V(endmark))
            i = sy.dma_start(out=y_out.rearrange("(t p) c -> p t c", p=128),
                             in_=x_sb[:].rearrange("p (t c) -> p t c", c=C))
            d_out.inc(i, 16)
            sy.wait_ge(d_out.h, d_out.n)

        @block.gpsimd
        def _(gp):
            if do_ffn1:
                i = gp.dma_start(out=wg_bf[0:SCH, :], in_=wg_i[0:SCH, :]); g_cast0.inc(i, 16)
                i = gp.dma_start(out=wu_bf[0:SCH, :], in_=wu_i[0:SCH, :]); g_cast0.inc(i, 16)
                i = gp.dma_start(out=wd_bf[:, 0:SCH], in_=wd_i[:, 0:SCH]); g_cast0.inc(i, 16)
                i = gp.dma_start(out=wg_bf[SCH:HID, :], in_=wg_i[SCH:HID, :]); g_cast.inc(i, 16)
                i = gp.dma_start(out=wu_bf[SCH:HID, :], in_=wu_i[SCH:HID, :]); g_cast.inc(i, 16)
                i = gp.dma_start(out=wd_bf[:, SCH:HID], in_=wd_i[:, SCH:HID]); g_cast.inc(i, 16)
            M("ffn_w_cast0", g_cast0)
            M("ffn_w_cast", g_cast)
            if do_attn:
                i = gp.dma_start(
                    out=qk_stage,
                    in_=zeros_c[0:1, :].broadcast_to([2048, 4096]))
                g_z.inc(i, 16)
                i = gp.dma_start(out=qkv_bf, in_=qkv_w); g_cast2.inc(i, 16)
                M("qkv_cast", g_cast2)
                i = gp.dma_start(out=proj_bf, in_=proj_w); g_cast3.inc(i, 16)
                i = gp.dma_start(out=pb_bf, in_=proj_b); g_cast3.inc(i, 16)
                M("proj_cast", g_cast3)
                gp.wait_ge(v_ms.h, V("scc_done"))
                i = gp.dma_start(out=sc_stage, in_=scc[:]); g_st.inc(i, 16)
                gp.wait_ge(g_st.h, g_st.n)
                for idx in range(2 * TT):
                    which, tt = divmod(idx, TT)
                    par = tt % 2
                    if idx >= 2:
                        pw, pt = divmod(idx - 2, TT)
                        gp.wait_ge(v_ms.h, V(f"rope{pw}_{pt}_used"))
                    i = gp.dma_start(
                        out=exs[:, par * C:(par + 1) * C],
                        in_=sc_stage[0:128, tt * 64:tt * 64 + 64]
                            .unsqueeze(1).broadcast_to([128, H, 64]))
                    g_exp.inc(i, 16)
                    i = gp.dma_start(
                        out=exc[:, par * C:(par + 1) * C],
                        in_=sc_stage[0:128, 1024 + tt * 64:1024 + tt * 64 + 64]
                            .unsqueeze(1).broadcast_to([128, H, 64]))
                    g_exp.inc(i, 16)
                    if idx >= 1:
                        pw, pt = divmod(idx - 1, TT)
                        M(f"exp{pw}_{pt}", g_exp)
                M(f"exp1_{TT-1}", g_exp)

        @block.tensor
        def _(te):
            def ffn_t(ffnname, lnname):
                te.wait_ge(d_ln.h, V(f"{lnname}_fm_loaded"))
                for sc in range(SCN):
                    te.wait_ge(d_w.h, V(f"{ffnname}_wgu{sc}"))
                    for hf in range(HFC):
                        for tg in range(4):
                            k = hf * 4 + tg
                            par = k % 2
                            gof, uof = par * 512, 1024 + par * 512
                            if k >= 2:
                                te.wait_ge(s_ms.h, V(f"{ffnname}_sil{sc}_{k-2}"))
                                te.wait_ge(v_ev.h, V(f"{ffnname}_h{sc}_{k-2}"))
                            elif sc > 0:
                                te.wait_ge(s_ms.h, V(f"{ffnname}_sil{sc-1}_{k+NK-2}"))
                                te.wait_ge(v_ev.h, V(f"{ffnname}_h{sc-1}_{k+NK-2}"))
                            last = None
                            for cc in range(CCn):
                                last = te.matmul(
                                    out=PS[:, gof:gof + 512],
                                    lhsT=wgT[:, cc * SCH + hf * 128:cc * SCH + (hf + 1) * 128],
                                    rhs=lnfm[:, cc * N + tg * 512:cc * N + (tg + 1) * 512],
                                    start=(cc == 0), stop=(cc == CCn - 1))
                            t_mm.inc(last)
                            M(f"{ffnname}_g{sc}_{k}", t_mm)
                            for cc in range(CCn):
                                last = te.matmul(
                                    out=PS[:, uof:uof + 512],
                                    lhsT=wuT[:, cc * SCH + hf * 128:cc * SCH + (hf + 1) * 128],
                                    rhs=lnfm[:, cc * N + tg * 512:cc * N + (tg + 1) * 512],
                                    start=(cc == 0), stop=(cc == CCn - 1))
                            t_mm.inc(last)
                            M(f"{ffnname}_u{sc}_{k}", t_mm)
                    M(f"{ffnname}_gu{sc}_done", t_mm)
                    te.wait_ge(d_wd.h, V(f"{ffnname}_w{sc}"))
                    for tt in range(TT):
                        par = tt % 2
                        dof = 2048 + par * 1024
                        if tt >= 2:
                            te.wait_ge(v_ev.h, V(f"{ffnname}_dnev{sc}_{tt-2}"))
                        elif sc > 0:
                            te.wait_ge(v_ev.h, V(f"{ffnname}_dnev{sc-1}_{tt+TT-2}"))
                        if tt == 0:
                            te.wait_ge(v_ev.h, V(f"{ffnname}_hsc{sc}_all"))
                        last = None
                        for hf in range(HFC):
                            for half in range(2):
                                last = te.matmul(
                                    out=PS[:, dof + half * 512:dof + (half + 1) * 512],
                                    lhsT=arena[:, hf * N + tt * 128:hf * N + (tt + 1) * 128],
                                    rhs=wdT[:, hf * C + half * 512:hf * C + (half + 1) * 512],
                                    start=(hf == 0), stop=(hf == HFC - 1))
                        t_mm.inc(last)
                        M(f"{ffnname}_dnmm{sc}_{tt}", t_mm)
                    M(f"{ffnname}_dn{sc}_done", t_mm)

            if do_ffn1:
                ffn_t("ffn1", "ln1")

            if do_attn:
                te.wait_ge(d_ln.h, V("ln2_fm_loaded"))
                for gidx in range(3):
                    te.wait_ge(d_qk.h, V(f"qkv_w{gidx}"))
                    for tt in range(TT):
                        par = tt % 2
                        pof = par * 1024
                        if gidx == 0 and tt < 2 and do_ffn1:
                            te.wait_ge(s_ms.h, V(f"ffn1_sil{SCN-1}_{NK-2+tt}"))
                            te.wait_ge(v_ev.h, V(f"ffn1_h{SCN-1}_{NK-2+tt}"))
                        if gidx == 0:
                            pass
                        if tt >= 2 or gidx > 0:
                            pidx = gidx * TT + tt - 2
                            pg, pt = divmod(pidx, TT)
                            te.wait_ge(v_ev.h, V(f"qkv_ev{pg}_{pt}"))
                        last = None
                        for half in range(2):
                            for cc in range(CCn):
                                last = te.matmul(
                                    out=PS[:, pof + half * 512:pof + (half + 1) * 512],
                                    lhsT=lnfm[:, cc * N + tt * 128:cc * N + (tt + 1) * 128],
                                    rhs=wqT[:, cc * C + half * 512:cc * C + (half + 1) * 512],
                                    start=(cc == 0), stop=(cc == CCn - 1))
                        t_mm.inc(last)
                        M(f"qkv_mm{gidx}_{tt}", t_mm)
                    M(f"qkv_g{gidx}_done", t_mm)

                te.wait_ge(v_ms.h, V("kk_ready"))
                te.wait_ge(d_c.h, V("prep_consts"))
                if do_ffn1:
                    te.wait_ge(v_ev.h, V(f"ffn1_dnev{SCN-1}_{TT-1}"))
                last = te.matmul(out=PS[:, 2048:2304], lhsT=onesb[:],
                                 rhs=smallb[0:1, 32:288], start=True, stop=True)
                for tt in range(TT):
                    last = te.matmul(out=PS[:, 2304 + tt * 16:2320 + tt * 16],
                                     lhsT=iosb[0:1, tt * 128:(tt + 1) * 128],
                                     rhs=smallb[0:1, 16:32], start=True, stop=True)
                t_mm.inc(last)
                M("mma_all", t_mm)

                te.wait_ge(d_m.h, V("qall_m_done"))
                te.wait_ge(v_ms.h, V(f"kall_{H-1}_copied"))
                te.wait_ge(v_ms.h, V("kall_ones"))
                te.wait_ge(v_ms.h, V(f"qall_{H-1}_copied"))
                for qg in range(NQG):
                    for h in range(H):
                        for kt in range(TT):
                            par = kt % 3
                            sof = par * 1024
                            if kt >= 3:
                                rq, rh, rkt = qg, h, kt - 3
                            elif not (qg == 0 and h == 0):
                                rq, rh = core_prev(qg, h)
                                rkt = kt + TT - 3
                            else:
                                rq = None
                            if rq is not None:
                                if is_above(rq, rkt):
                                    te.wait_ge(s_w.h, V(f"w_{rq}_{rh}_{rkt}"))
                                else:
                                    te.wait_ge(v_al.h, V(f"al_{rq}_{rh}_{rkt}"))
                            last = None
                            for half in range(2):
                                last = te.matmul(
                                    out=PS[:, sof + half * 512:sof + (half + 1) * 512],
                                    lhsT=kall[:, h * N + kt * 128:h * N + (kt + 1) * 128],
                                    rhs=qall[:, h * N + qg * QG + half * 512:
                                             h * N + qg * QG + (half + 1) * 512],
                                    start=True, stop=True)
                            t_mm.inc(last)
                            M(f"sc_{qg}_{h}_{kt}", t_mm)
                            if kt >= 1:
                                ktp = kt - 1
                                te.wait_ge(s_w.h, V(f"w_{qg}_{h}_{ktp}"))
                                if ktp == 0 and not (qg == 0 and h == 0):
                                    pq, ph = core_prev(qg, h)
                                    te.wait_ge(v_ev.h, V(f"avev_{pq}_{ph}"))
                                    te.wait_ge(s_ev.h, V(f"lev_{pq}_{ph}"))
                                wpar = ktp % 3
                                last = None
                                for half in range(2):
                                    last = te.matmul(
                                        out=PS[0:65, 3072 + half * 512:3072 + (half + 1) * 512],
                                        lhsT=vP[:, ktp * (H * 65) + h * 65:
                                                ktp * (H * 65) + (h + 1) * 65],
                                        rhs=wex[:, wpar * QG + half * 512:
                                                wpar * QG + (half + 1) * 512],
                                        start=(ktp == 0), stop=False)
                                t_av.inc(last)
                                M(f"av_{qg}_{h}_{ktp}", t_av)
                        ktp = TT - 1
                        te.wait_ge(s_w.h, V(f"w_{qg}_{h}_{ktp}"))
                        wpar = ktp % 3
                        last = None
                        for half in range(2):
                            last = te.matmul(
                                out=PS[0:65, 3072 + half * 512:3072 + (half + 1) * 512],
                                lhsT=vP[:, ktp * (H * 65) + h * 65:
                                        ktp * (H * 65) + (h + 1) * 65],
                                rhs=wex[:, wpar * QG + half * 512:
                                        wpar * QG + (half + 1) * 512],
                                start=False, stop=True)
                        t_av.inc(last)
                        M(f"av_{qg}_{h}_done", t_av)
                M("core_av_done", t_av)

                te.wait_ge(d_pj.h, V("projw_loaded"))
                te.wait_ge(v_ms.h, V(f"prin_{CCn-1}_done"))
                for tt in range(TT):
                    par = tt % 2
                    pof = par * 1024
                    if tt >= 2:
                        te.wait_ge(v_ev.h, V(f"projev_{tt-2}"))
                    last = None
                    for half in range(2):
                        for cc in range(CCn):
                            last = te.matmul(
                                out=PS[:, pof + half * 512:pof + (half + 1) * 512],
                                lhsT=prin[:, cc * N + tt * 128:cc * N + (tt + 1) * 128],
                                rhs=prjT[:, cc * C + half * 512:cc * C + (half + 1) * 512],
                                start=(cc == 0), stop=False)
                        last = te.matmul(
                            out=PS[:, pof + half * 512:pof + (half + 1) * 512],
                            lhsT=onesb[:],
                            rhs=pbs[0:1, half * 512:(half + 1) * 512],
                            start=False, stop=True)
                    t_mm.inc(last)
                    M(f"projmm_{tt}", t_mm)

            if do_ffn2:
                ffn_t("ffn2", "ln3")

        @block.vector
        def _(ve):
            def ln_v(lnname, waitsem, waitval):
                ve.wait_ge(waitsem.h, waitval)
                last = None
                for tt in range(TT):
                    for hlf in range(2):
                        last = ve.bn_stats(
                            out=bn6[:, tt * 12 + hlf * 6:tt * 12 + (hlf + 1) * 6],
                            in_=x_sb[:, tt * C + hlf * 512:tt * C + (hlf + 1) * 512])
                v_ms.inc(last); M(f"{lnname}_stats", v_ms)
                ve.wait_ge(v_ms.h, V(f"{lnname}_stats"))
                for tt in range(TT):
                    last = ve.bn_aggr(out=mvb[:, tt * 4:tt * 4 + 2],
                                      in_=bn6[:, tt * 12:(tt + 1) * 12])
                v_ms.inc(last); M(f"{lnname}_aggr", v_ms)
                ve.wait_ge(v_ms.h, V(f"{lnname}_aggr"))
                mv = mvb[:].rearrange("p (t f) -> p t f", f=4)
                last = ve.tensor_scalar(out=mv[:, :, 2:3], in0=mv[:, :, 1:2],
                                        scalar1=EPS, scalar2=None, op0=OP.add)
                v_ms.inc(last); M(f"{lnname}_veps", v_ms)
                ve.wait_ge(s_ms.h, V(f"{lnname}_sqrt"))
                last = ve.reciprocal(out=mv[:, :, 2:3], in_=mv[:, :, 3:4])
                v_ms.inc(last); M(f"{lnname}_rstd", v_ms)
                ve.wait_ge(v_ms.h, V(f"{lnname}_rstd"))
                for tt in range(TT):
                    last = ve.tensor_scalar(out=arena[:, tt * C:(tt + 1) * C],
                                            in0=x_sb[:, tt * C:(tt + 1) * C],
                                            scalar1=mvb[:, tt * 4:tt * 4 + 1],
                                            scalar2=mvb[:, tt * 4 + 2:tt * 4 + 3],
                                            op0=OP.subtract, op1=OP.mult)
                v_ms.inc(last); M(f"{lnname}_tok_done", v_ms)

            def ffn_v(ffnname):
                last = None
                for sc in range(SCN):
                    for hf in range(HFC):
                        for tg in range(4):
                            k = hf * 4 + tg
                            par = k % 2
                            uof = 1024 + par * 512
                            ve.wait_ge(s_ms.h, V(f"{ffnname}_sil{sc}_{k}"))
                            ve.wait_ge(t_mm.h, V(f"{ffnname}_u{sc}_{k}"))
                            if k == 0 and sc > 0:
                                ve.wait_ge(t_mm.h, V(f"{ffnname}_dn{sc-1}_done"))
                            last = ve.tensor_tensor(
                                out=arena[:, hf * N + tg * 512:hf * N + (tg + 1) * 512],
                                in0=asil[:, par * 512:(par + 1) * 512],
                                in1=PS[:, uof:uof + 512], op=OP.mult)
                            v_ev.inc(last)
                            M(f"{ffnname}_h{sc}_{k}", v_ev)
                    M(f"{ffnname}_hsc{sc}_all", v_ev)
                    for tt in range(TT):
                        par = tt % 2
                        dof = 2048 + par * 1024
                        ve.wait_ge(t_mm.h, V(f"{ffnname}_dnmm{sc}_{tt}"))
                        last = ve.scalar_tensor_tensor(
                            out=x_sb[:, tt * C:(tt + 1) * C],
                            in0=PS[:, dof:dof + 1024], scalar=ALPHA,
                            in1=x_sb[:, tt * C:(tt + 1) * C],
                            op0=OP.mult, op1=OP.add)
                        v_ev.inc(last)
                        M(f"{ffnname}_dnev{sc}_{tt}", v_ev)
                M(f"{ffnname}_x_done", v_ev)

            if upto != "attnonly":
                ln_v("ln1", d_x, 16)
            if upto == "ln1":
                ve.wait_ge(v_ms.h, V("ln1_tok_done"))
                last = None
                for tt in range(TT):
                    last = ve.tensor_copy(out=x_sb[:, tt * C:(tt + 1) * C],
                                          in_=arena[:, tt * C:(tt + 1) * C])
                v_ms.inc(last); M("ln1_out_ready", v_ms)
                return

            if do_ffn1:
                ffn_v("ffn1")
                ln_v("ln2", v_ev, V("ffn1_x_done"))
            elif upto == "attnonly":
                ln_v("ln2", d_x, 16)

            if do_attn:
                ve.wait_ge(d_c.h, V("prep_consts"))
                last = ve.tensor_scalar(out=smallf[0:1, 96:128], in0=smallf[0:1, 0:32],
                                        scalar1=float(-math.log(10000.0) / 32.0),
                                        scalar2=None, op0=OP.mult)
                v_ms.inc(last); M("frm_done", v_ms)
                ve.wait_ge(d_fr.h, V("frb_loaded"))
                for tt in range(TT):
                    last = ve.tensor_scalar(out=smallf[:, 33 + tt:34 + tt],
                                            in0=smallf[:, 32:33],
                                            scalar1=float(128 * tt), scalar2=INV2PI,
                                            op0=OP.add, op1=OP.mult)
                v_ms.inc(last); M("tv_done", v_ms)
                ve.wait_ge(v_ms.h, V("tv_done"))
                for tt in range(TT):
                    last = ve.tensor_scalar(out=t1f[:, tt * 32:(tt + 1) * 32],
                                            in0=t2f[:, tt * 32:(tt + 1) * 32],
                                            scalar1=smallf[:, 33 + tt:34 + tt],
                                            scalar2=None, op0=OP.mult)
                v_ms.inc(last); M("ang_done", v_ms)
                ve.wait_ge(v_ms.h, V("ang_done"))
                last = ve.tensor_scalar(out=iscr[:], in0=t1f[:, 0:512],
                                        scalar1=0.5, scalar2=None, op0=OP.add)
                v_ms.inc(last); M("rr1", v_ms)
                ve.wait_ge(v_ms.h, V("rr1"))
                last = ve.tensor_copy(out=t1f[:, 512:1024], in_=iscr[:])
                v_ms.inc(last); M("rr2", v_ms)
                ve.wait_ge(v_ms.h, V("rr2"))
                last = ve.tensor_sub(out=t2f[:, 512:1024], in0=t1f[:, 0:512],
                                     in1=t1f[:, 512:1024])
                v_ms.inc(last); M("rsin0", v_ms)
                ve.wait_ge(v_ms.h, V("rsin0"))
                last = ve.tensor_scalar(out=t2f[:, 512:1024], in0=t2f[:, 512:1024],
                                        scalar1=-0.49999, scalar2=0.49999,
                                        op0=OP.max, op1=OP.min)
                v_ms.inc(last); M("rsin_done", v_ms)
                ve.wait_ge(v_ms.h, V("rsin_done"))
                last = ve.tensor_scalar(out=t1f[:, 0:512], in0=t1f[:, 0:512],
                                        scalar1=0.25, scalar2=None, op0=OP.add)
                v_ms.inc(last); M("rc0", v_ms)
                ve.wait_ge(v_ms.h, V("rc0"))
                last = ve.tensor_scalar(out=iscr[:], in0=t1f[:, 0:512],
                                        scalar1=0.5, scalar2=None, op0=OP.add)
                v_ms.inc(last); M("rc1", v_ms)
                ve.wait_ge(v_ms.h, V("rc1"))
                last = ve.tensor_copy(out=t1f[:, 512:1024], in_=iscr[:])
                v_ms.inc(last); M("rc2", v_ms)
                ve.wait_ge(v_ms.h, V("rc2"))
                last = ve.tensor_sub(out=t1f[:, 512:1024], in0=t1f[:, 0:512],
                                     in1=t1f[:, 512:1024])
                v_ms.inc(last); M("rcos_done", v_ms)
                ve.wait_ge(s_ms.h, V("sincos_eval"))
                sv = t1f[:, 0:512].rearrange("p (t j) -> p t j", j=32)
                cv = t2f[:, 0:512].rearrange("p (t j) -> p t j", j=32)
                sccv = scc[:].rearrange("p (k t j) -> p k t j", k=2, j=64)
                ve.tensor_scalar(out=sccv[:, 0:1, :, 0:32].squeeze(1), in0=sv,
                                 scalar1=-INVSQRT8, scalar2=None, op0=OP.mult)
                ve.tensor_scalar(out=sccv[:, 0:1, :, 32:64].squeeze(1), in0=sv,
                                 scalar1=INVSQRT8, scalar2=None, op0=OP.mult)
                ve.tensor_scalar(out=sccv[:, 1:2, :, 0:32].squeeze(1), in0=cv,
                                 scalar1=INVSQRT8, scalar2=None, op0=OP.mult)
                last = ve.tensor_scalar(out=sccv[:, 1:2, :, 32:64].squeeze(1), in0=cv,
                                        scalar1=INVSQRT8, scalar2=None, op0=OP.mult)
                v_ms.inc(last); M("scc_done", v_ms)

                ve.wait_ge(d_pk.h, 16)
                last = None
                for tt in range(TT):
                    par = tt % 2
                    pof = par * 1024
                    ve.wait_ge(t_mm.h, V(f"qkv_mm0_{tt}"))
                    vv = vP[:, tt * H * 65:(tt + 1) * H * 65].rearrange(
                        "p (h d) -> p h d", d=65)
                    ve.memset(vv[:, :, 64:65].squeeze(2), 1.0)
                    pv = PS[:, pof:pof + 1024].rearrange("p (h d) -> p h d", d=64)
                    last = ve.tensor_copy(out=vv[:, :, 0:64], in_=pv)
                    v_ev.inc(last)
                    M(f"qkv_ev0_{tt}", v_ev)

                for which, gsel, dst, n2 in ((0, 1, ktok, nk2), (1, 2, arena, nq2)):
                    if which == 1:
                        ve.wait_ge(d_ln.h, V("ln2_fm_loaded"))
                    for tt in range(TT):
                        par = tt % 2
                        pof = par * 1024
                        ve.wait_ge(t_mm.h, V(f"qkv_mm{gsel}_{tt}"))
                        ve.wait_ge(g_exp.h, V(f"exp{which}_{tt}"))
                        P = PS[:, pof:pof + 1024]
                        cosx = exc[:, par * C:(par + 1) * C]
                        sinx = exs[:, par * C:(par + 1) * C]
                        last = ve.tensor_tensor(out=t1f[:], in0=P, in1=cosx, op=OP.mult)
                        v_ms.inc(last); M(f"ropeA{which}_{tt}", v_ms)
                        Pv = P.rearrange("p (h d) -> p h d", d=64)
                        sxv = sinx.rearrange("p (h d) -> p h d", d=64)
                        t2v = t2f[:].rearrange("p (h d) -> p h d", d=64)
                        ve.tensor_tensor(out=t2v[:, :, 0:32], in0=Pv[:, :, 32:64],
                                         in1=sxv[:, :, 0:32], op=OP.mult)
                        last = ve.tensor_tensor(out=t2v[:, :, 32:64], in0=Pv[:, :, 0:32],
                                                in1=sxv[:, :, 32:64], op=OP.mult)
                        v_ms.inc(last)
                        M(f"rope{which}_{tt}_used", v_ms)
                        ve.wait_ge(v_ms.h, V(f"rope{which}_{tt}_used"))
                        last = ve.tensor_add(out=dst[:, tt * C:(tt + 1) * C],
                                             in0=t1f[:], in1=t2f[:])
                        v_ev.inc(last)
                        M(f"qkv_ev{gsel}_{tt}", v_ev)
                        ve.wait_ge(v_ev.h, V(f"qkv_ev{gsel}_{tt}"))
                        last = ve.tensor_mul(out=sqb[:], in0=dst[:, tt * C:(tt + 1) * C],
                                             in1=dst[:, tt * C:(tt + 1) * C])
                        v_ms.inc(last); M(f"sq{which}_{tt}", v_ms)
                        ve.wait_ge(v_ms.h, V(f"sq{which}_{tt}"))
                        last = ve.tensor_reduce(
                            out=n2[:, tt * H:(tt + 1) * H],
                            in_=sqb[:].rearrange("p (h d) -> p h d", d=64),
                            axis=AX.X, op=OP.add)
                        v_ms.inc(last); M(f"n2{which}_{tt}", v_ms)
                        if which == 1:
                            ve.wait_ge(v_ms.h, V(f"n2{which}_{tt}"))
                            last = ve.tensor_mul(out=sqb[:],
                                                 in0=arena[:, tt * C:(tt + 1) * C],
                                                 in1=ktok[:, tt * C:(tt + 1) * C])
                            v_ms.inc(last); M(f"sqd_{tt}", v_ms)
                            ve.wait_ge(v_ms.h, V(f"sqd_{tt}"))
                            last = ve.tensor_reduce(
                                out=dtok[:, tt * H:(tt + 1) * H],
                                in_=sqb[:].rearrange("p (h d) -> p h d", d=64),
                                axis=AX.X, op=OP.add, negate=True)
                            v_ms.inc(last); M(f"negd_{tt}", v_ms)
                    M("rope_k_done" if which == 0 else "rope_q_done", v_ms)

                last = ve.tensor_reduce(out=smallf[:, 128:144],
                                        in_=nk2[:].rearrange("p (t h) -> p h t", h=H),
                                        axis=AX.X, op=OP.max)
                v_ms.inc(last); M("kmax0", v_ms)
                ve.wait_ge(d_kx.h, 16)
                last = ve.tensor_reduce(
                    out=smallf[0:1, 128:144],
                    in_=lsb[0:1, 0:2048].rearrange("a (p h) -> a h p", h=16),
                    axis=AX.X, op=OP.max)
                v_ms.inc(last); M("kmax1", v_ms)
                ve.wait_ge(v_ms.h, V("kmax1"))
                last = ve.tensor_scalar(out=smallb[0:1, 32:48], in0=smallf[0:1, 128:144],
                                        scalar1=-0.5, scalar2=75.0, op0=OP.mult, op1=OP.add)
                v_ms.inc(last); M("kk1", v_ms)
                last = ve.tensor_scalar(out=smallb[0:1, 16:32], in0=slsb[0:1, :],
                                        scalar1=-8.0, scalar2=0.0, op0=OP.mult, op1=OP.max)
                v_ms.inc(last); M("kk2", v_ms)
                rep = 16
                while rep < 256:
                    ve.wait_ge(v_ms.h, v_ms.n)
                    last = ve.tensor_copy(out=smallb[0:1, 32 + rep:32 + 2 * rep],
                                          in_=smallb[0:1, 32:32 + rep])
                    v_ms.inc(last)
                    M(f"brow_{rep}", v_ms)
                    rep *= 2
                M("kk_ready", v_ms)

                ve.wait_ge(t_mm.h, V("mma_all"))
                last = ve.scalar_tensor_tensor(
                    out=mnegall[:], in0=nq2[:], scalar=-0.5,
                    in1=PS[:, 2048:2304], op0=OP.mult, op1=OP.add)
                v_ms.inc(last); M("mneg1", v_ms)
                ve.wait_ge(v_ms.h, V("mneg1"))
                last = ve.tensor_tensor(out=mnegall[:], in0=mnegall[:], in1=dtok[:],
                                        op=OP.min)
                v_ms.inc(last); M("mneg2", v_ms)
                ve.wait_ge(v_ms.h, V("mneg2"))
                last = ve.tensor_tensor(out=mnegall[:], in0=mnegall[:],
                                        in1=PS[:, 2304:2560], op=OP.subtract)
                v_ms.inc(last); M(f"mneg_{TT-1}", v_ms)
                mn = mnegall[:].rearrange("p (t h) -> p t h", h=H)
                mspv = msp[:].rearrange("p (t h s) -> p t h s", h=H, s=8)
                hi_b = mspv[:, :, :, 0:1].squeeze(3)
                mid_b = mspv[:, :, :, 1:2].squeeze(3)
                lo_b = mspv[:, :, :, 2:3].squeeze(3)
                hi_f = t1f[:, 0:256].rearrange("p (t h) -> p t h", h=H)
                d1 = t1f[:, 256:512].rearrange("p (t h) -> p t h", h=H)
                mid_f = t1f[:, 512:768].rearrange("p (t h) -> p t h", h=H)
                d2 = t1f[:, 768:1024].rearrange("p (t h) -> p t h", h=H)
                ve.wait_ge(v_ms.h, V(f"mneg_{TT-1}"))
                last = ve.tensor_copy(out=hi_b, in_=mn)
                v_ms.inc(last); M("mhi", v_ms)
                ve.wait_ge(v_ms.h, V("mhi"))
                last = ve.tensor_copy(out=hi_f, in_=hi_b)
                v_ms.inc(last); M("mhif", v_ms)
                ve.wait_ge(v_ms.h, V("mhif"))
                last = ve.tensor_sub(out=d1, in0=mn, in1=hi_f)
                v_ms.inc(last); M("md1", v_ms)
                ve.wait_ge(v_ms.h, V("md1"))
                last = ve.tensor_copy(out=mid_b, in_=d1)
                v_ms.inc(last); M("mmid", v_ms)
                ve.wait_ge(v_ms.h, V("mmid"))
                last = ve.tensor_copy(out=mid_f, in_=mid_b)
                v_ms.inc(last); M("mmidf", v_ms)
                ve.wait_ge(v_ms.h, V("mmidf"))
                last = ve.tensor_sub(out=d2, in0=d1, in1=mid_f)
                v_ms.inc(last); M("md2", v_ms)
                ve.wait_ge(v_ms.h, V("md2"))
                ve.tensor_copy(out=lo_b, in_=d2)
                last = ve.memset(mspv[:, :, :, 3:8], 0.0)
                v_ms.inc(last); M("msp_done", v_ms)

                ve.wait_ge(d_st.h, V("q_staged"))
                for h in range(H):
                    ve.wait_ge(d_qk.h, V(f"kT_{h}"))
                    last = ve.tensor_copy(out=kall[0:64, h * N:(h + 1) * N],
                                          in_=atmp[0:64, (h % 2) * N:(h % 2 + 1) * N])
                    v_ms.inc(last)
                    M(f"kall_{h}_copied", v_ms)
                last = ve.memset(kall[64:67, :], 1.0)
                v_ms.inc(last); M("kall_ones", v_ms)
                for h in range(H):
                    ve.wait_ge(d_qk.h, V(f"qT_{h}"))
                    last = ve.tensor_copy(out=qall[0:64, h * N:(h + 1) * N],
                                          in_=atmp[0:64, (h % 2) * N:(h % 2 + 1) * N])
                    v_ms.inc(last)
                    M(f"qall_{h}_copied", v_ms)

                for qg in range(NQG):
                    ve.wait_ge(d_c.h, V(f"mstw_{qg}"))
                    for h in range(H):
                        for kt in range(TT):
                            if is_above(qg, kt):
                                continue
                            par = kt % 3
                            sof = par * 1024
                            cs = max(0, kt * 128 - qg * QG)
                            ve.wait_ge(t_mm.h, V(f"sc_{qg}_{h}_{kt}"))
                            if kt >= 3:
                                ve.wait_ge(s_w.h, V(f"w_{qg}_{h}_{kt-3}"))
                            elif not (qg == 0 and h == 0):
                                pq, ph = core_prev(qg, h)
                                ve.wait_ge(s_w.h, V(f"w_{pq}_{ph}_{kt+TT-3}"))
                            cst = 2047 - kt * 128
                            last = ve.scalar_tensor_tensor(
                                out=sten[:, par * QG + cs:(par + 1) * QG],
                                in0=mstw[:, cst + cs:cst + QG],
                                scalar=slsb[:, h:h + 1],
                                in1=PS[:, sof + cs:sof + QG],
                                op0=OP.mult, op1=OP.add)
                            v_al.inc(last)
                            M(f"al_{qg}_{h}_{kt}", v_al)
                        ve.wait_ge(t_av.h, V(f"av_{qg}_{h}_done"))
                        if not (qg == 0 and h == 0):
                            pq, ph = core_prev(qg, h)
                            ve.wait_ge(d_at.h, V(f"atst_{pq}_{ph}"))
                        last = ve.tensor_copy(out=asb[:], in_=PS[0:64, 3072:4096])
                        v_ev.inc(last)
                        M(f"avev_{qg}_{h}", v_ev)
                    M(f"alibi_{qg}_done", v_al)

                ve.wait_ge(d_pj.h, V("lsb_loaded"))
                last = ve.reciprocal(out=rsb[:], in_=lsb[:])
                v_ms.inc(last); M("recip_done", v_ms)

                for cc in range(CCn):
                    ve.wait_ge(d_pj.h, V(f"prin_{cc}_loaded"))
                    last = ve.tensor_tensor(out=prin[:, cc * N:(cc + 1) * N],
                                            in0=praw[:], in1=rlb[:], op=OP.mult)
                    v_ms.inc(last)
                    M(f"prin_{cc}_done", v_ms)
                for tt in range(TT):
                    par = tt % 2
                    pof = par * 1024
                    ve.wait_ge(t_mm.h, V(f"projmm_{tt}"))
                    last = ve.tensor_tensor(out=x_sb[:, tt * C:(tt + 1) * C],
                                            in0=x_sb[:, tt * C:(tt + 1) * C],
                                            in1=PS[:, pof:pof + 1024], op=OP.add)
                    v_ev.inc(last)
                    M(f"projev_{tt}", v_ev)
                M("proj_x_done", v_ev)

            if do_ffn2:
                ln_v("ln3", v_ev, V("proj_x_done"))
                ffn_v("ffn2")

        @block.scalar
        def _(sc):
            def ln_s(lnname):
                sc.wait_ge(v_ms.h, V(f"{lnname}_veps"))
                mv = mvb[:].rearrange("p (t f) -> p t f", f=4)
                last = sc.activation(out=mv[:, :, 3:4], in_=mv[:, :, 2:3], func=AF.Sqrt)
                s_ms.inc(last)
                M(f"{lnname}_sqrt", s_ms)

            def ffn_s(ffnname):
                for scn in range(SCN):
                    for hf in range(HFC):
                        for tg in range(4):
                            k = hf * 4 + tg
                            par = k % 2
                            gof = par * 512
                            sc.wait_ge(t_mm.h, V(f"{ffnname}_g{scn}_{k}"))
                            if k >= 2:
                                sc.wait_ge(v_ev.h, V(f"{ffnname}_h{scn}_{k-2}"))
                            elif scn > 0:
                                sc.wait_ge(v_ev.h, V(f"{ffnname}_h{scn-1}_{k+NK-2}"))
                            last = sc.activation(out=asil[:, par * 512:(par + 1) * 512],
                                                 in_=PS[:, gof:gof + 512],
                                                 func=AF.Silu, scale=S127)
                            s_ms.inc(last)
                            M(f"{ffnname}_sil{scn}_{k}", s_ms)

            if upto != "attnonly":
                ln_s("ln1")
            if upto == "ln1":
                return
            if do_ffn1:
                ffn_s("ffn1")
                ln_s("ln2")
            elif upto == "attnonly":
                ln_s("ln2")
            if do_attn:
                sc.wait_ge(v_ms.h, V("frm_done"))
                last = sc.activation(out=smallf[0:1, 64:96], in_=smallf[0:1, 96:128],
                                     func=AF.Exp)
                s_ms.inc(last); M("fr_done", s_ms)
                sc.wait_ge(v_ms.h, V("rcos_done"))
                last = sc.activation(out=t1f[:, 0:512], in_=t2f[:, 512:1024],
                                     func=AF.Sin, scale=TWOPI)
                s_ms.inc(last); M("sin_eval", s_ms)
                last = sc.activation(out=t2f[:, 0:512], in_=t1f[:, 512:1024],
                                     func=AF.Sin, scale=TWOPI)
                s_ms.inc(last); M("sincos_eval", s_ms)
                qkv_view = qk_stage.rearrange("(t p) f -> p t f", p=128) \
                    .rearrange("p t (h c) -> p t h c", c=128)
                sc.wait_ge(g_z.h, 16)
                sc.wait_ge(v_ms.h, V("rope_k_done"))
                for tt in range(TT):
                    i = sc.dma_start(
                        out=qkv_view[:, tt:tt + 1, H:2 * H, 0:64].squeeze(1),
                        in_=ktok[:, tt * C:(tt + 1) * C]
                            .rearrange("p (h d) -> p h d", d=64))
                    d_st.inc(i, 16)
                M("k_staged", d_st)
                sc.wait_ge(d_st.h, d_st.n)
                sc.wait_ge(v_ms.h, V("rope_q_done"))
                for tt in range(TT):
                    i = sc.dma_start(
                        out=qkv_view[:, tt:tt + 1, 0:H, 0:64].squeeze(1),
                        in_=arena[:, tt * C:(tt + 1) * C]
                            .rearrange("p (h d) -> p h d", d=64))
                    d_st.inc(i, 16)
                M("q_staged", d_st)
                for qg in range(NQG):
                    for h in range(H):
                        for kt in range(TT):
                            par = kt % 3
                            cs = max(0, kt * 128 - qg * QG)
                            if kt >= 3:
                                sc.wait_ge(t_av.h, V(f"av_{qg}_{h}_{kt-3}"))
                            elif not (qg == 0 and h == 0):
                                pq, ph = core_prev(qg, h)
                                if kt + TT - 3 < TT - 1:
                                    sc.wait_ge(t_av.h, V(f"av_{pq}_{ph}_{kt+TT-3}"))
                                else:
                                    sc.wait_ge(t_av.h, V(f"av_{pq}_{ph}_done"))
                            if is_above(qg, kt):
                                sc.wait_ge(t_mm.h, V(f"sc_{qg}_{h}_{kt}"))
                                last = sc.activation(
                                    out=wex[:, par * QG:(par + 1) * QG],
                                    in_=PS[:, par * 1024:par * 1024 + QG], func=AF.Exp)
                            else:
                                sc.wait_ge(v_al.h, V(f"al_{qg}_{h}_{kt}"))
                                if cs > 0:
                                    sc.activation(
                                        out=wex[:, par * QG:par * QG + cs],
                                        in_=PS[:, par * 1024:par * 1024 + cs],
                                        func=AF.Exp)
                                last = sc.activation(
                                    out=wex[:, par * QG + cs:(par + 1) * QG],
                                    in_=sten[:, par * QG + cs:(par + 1) * QG],
                                    func=AF.Exp)
                            s_w.inc(last)
                            M(f"w_{qg}_{h}_{kt}", s_w)
                        sc.wait_ge(t_av.h, V(f"av_{qg}_{h}_done"))
                        if not (qg == 0 and h == 0):
                            pq, ph = core_prev(qg, h)
                            sc.wait_ge(d_at.h, V(f"atst_{pq}_{ph}"))
                        last = sc.copy(out=lrow[64:65, :], in_=PS[64:65, 3072:4096])
                        s_ev.inc(last)
                        M(f"lev_{qg}_{h}", s_ev)
            if do_ffn2:
                ln_s("ln3")
                ffn_s("ffn2")

    ctx.close()
    return nc


def _consts():
    import ml_dtypes
    bf = ml_dtypes.bfloat16
    ones_c = np.ones((1, 128), dtype=bf)
    zeros_c = np.zeros((1, 4096), dtype=bf)
    iota_f = np.arange(N, dtype=np.float32).reshape(1, N)
    iota_b = iota_f.astype(bf)
    iota_cl = np.arange(128, dtype=np.float32).reshape(128, 1)
    i = np.arange(128, dtype=np.float32).reshape(128, 1)
    c = np.arange(2 * N - 1, dtype=np.float32).reshape(1, 2 * N - 1)
    mst = 8.0 * np.minimum(i - c + (N - 1), 0.0)
    return dict(ones_c=ones_c, zeros_c=zeros_c, iota_f=iota_f, iota_b=iota_b,
                iota_cl=iota_cl, mst_c=np.ascontiguousarray(mst, dtype=np.float32))


_CACHE = {}


def _get_nc(upto="full"):
    if upto not in _CACHE:
        tape = Tape()
        build_program(tape, upto)
        tape.record = False
        _CACHE[upto] = build_program(tape, upto)
    return _CACHE[upto]


def kernel(L0_attn_x, L0_attn_qkv_w, L0_attn_proj_w, L0_attn_proj_b,
           L0_attn_slopes, L0_ffn_w_gate, L0_ffn_w_up, L0_ffn_w_down,
           _upto="full", _cores=8):
    nc = _get_nc(_upto)
    consts = _consts()
    base = dict(
        qkv_w=np.ascontiguousarray(L0_attn_qkv_w, dtype=np.float32),
        proj_w=np.ascontiguousarray(L0_attn_proj_w, dtype=np.float32),
        proj_b=np.ascontiguousarray(np.reshape(np.asarray(L0_attn_proj_b,
                                                          dtype=np.float32), (1, C))),
        slopes_b=np.ascontiguousarray(
            np.broadcast_to(np.reshape(np.asarray(L0_attn_slopes, dtype=np.float32),
                                       (1, H)), (128, H))),
        wg_i=np.ascontiguousarray(L0_ffn_w_gate, dtype=np.int32),
        wu_i=np.ascontiguousarray(L0_ffn_w_up, dtype=np.int32),
        wd_i=np.ascontiguousarray(L0_ffn_w_down, dtype=np.int32),
        **consts,
    )
    in_maps = []
    for b in range(_cores):
        m = dict(base)
        m["x"] = np.ascontiguousarray(L0_attn_x[b], dtype=np.float32)
        in_maps.append(m)
    res = run_bass_kernel_spmd(nc, in_maps, core_ids=list(range(_cores)))
    out = np.stack([np.asarray(r["y"]) for r in res.results], axis=0)
    if _cores < 8:
        pad = np.zeros((8 - _cores, N, C), dtype=np.float32)
        out = np.concatenate([out, pad], axis=0)
    return out
